# version 05
# baseline (speedup 1.0000x reference)
import contextlib
import numpy as np
import concourse.bass as bass
import concourse.mybir as mybir
from concourse.bass_utils import run_bass_kernel_spmd
from concourse.bass_core_rust import IndirectOffsetOnAxis

F32 = mybir.dt.float32
BF16 = mybir.dt.bfloat16
U32 = mybir.dt.uint32
AF = mybir.ActivationFunctionType
ALU = mybir.AluOpType
AXX = mybir.AxisListType.X
RSQRT128 = 0.08838834764831845
SQRT128 = 11.313708498984761
CEXP = -16.0
CRW = -24.0


class Cfg:
    def __init__(s, ncores, S, D, R, FFN, phases=99):
        s.ncores, s.S, s.D, s.R, s.FFN = ncores, S, D, R, FFN
        s.B = ncores // 2
        s.H, s.HD = D // 128, 128
        s.TOPK = S // 4
        s.OV = (S // 8) + (S // 16)
        assert s.OV % 128 == 0 and (s.TOPK - s.OV) % 128 == 0
        s.Tc = S // 2
        s.TH = s.Tc + 6
        s.THp = ((s.TH + 15) // 16) * 16
        s.Dc, s.Rc, s.Fc = D // 128, R // 128, FFN // 128
        s.OVc, s.Kc = s.OV // 128, s.TOPK // 128
        s.Tt = s.Tc // 128
        s.GRID = 64
        s.phases = phases
        s.ng2_force = None
        s.dbg_scr = ()
        s.dbg_notail = False

    def segs(s, n, w=512):
        r, o = [], 0
        while o < n:
            r.append((o, min(w, n - o)))
            o += w
        return r


class Sem:
    def __init__(s, es, nc, name):
        s.h = es.enter_context(nc.semaphore(name))
        s.n = 0

    def tag(s, ins, k=1):
        ins.then_inc(s.h, k)
        s.n += k
        return s.n


class B:
    def __init__(self, c: Cfg):
        self.c = c
        self.nc = bass.Bass("TRN2", target_bir_lowering=False,
                            detect_race_conditions=False)
        self.es = contextlib.ExitStack()
        self.pes = None
        self.inv_d = 1.0 / c.D
        cv = [0.0, 1.0, -1.0, -0.5, 0.5, 2.0, -2.0, -8.0, CEXP, CRW, 1e-5, 1e-6,
              1e6, -1e6, RSQRT128, SQRT128, self.inv_d, float(c.TOPK),
              float(c.OV - 1), 1.0 / 65.0, 64.0, 1.0 / c.OV, float(c.OV),
              float(2 * c.OV - c.TOPK), 1.5]
        cv += [float(128 * i) for i in range(max(c.OVc, c.Kc, c.Tt, 2))]
        self.consts = sorted(set(cv))
        self.cmap = {v: i for i, v in enumerate(self.consts)}
        self._declare()

    def C(self, v):
        v = float(v)
        return self.cst[:, self.cmap[v]:self.cmap[v] + 1]

    def sp(self, v):
        v.tensor_copy(self.drain_sb[0:1, 6:7], self.drain_sb[0:1, 0:1])

    def tagY(self, v):
        self.sp(v)
        self.sp(v)
        self.sp(v)
        self.cY.tag(v.tensor_copy(self.drain_sb[0:1, 1:2], self.drain_sb[0:1, 0:1]))

    def Cp(self, v, rows):
        v = float(v)
        return self.cst[0:rows, self.cmap[v]:self.cmap[v] + 1]

    def sb(self, name, p, f, dt):
        return self.pes.enter_context(self.nc.sbuf_tensor(name, [p, f], dt))

    def phase_open(self):
        assert self.pes is None
        self.pes = contextlib.ExitStack()

    def phase_close(self):
        self.pes.close()
        self.pes = None
        self.fence = {s: s.n for s in self.sems}

    def fwait(self, eng):
        for s, n in self.fence.items():
            if n > 0:
                eng.wait_ge(s.h, n)

    def _declare(self):
        nc, c, es = self.nc, self.c, self.es

        def din(name, shape, dt=F32):
            return nc.dram_tensor(name, list(shape), dt, kind="ExternalInput").ap()

        self.dbg_twins = []

        def scr(name, shape, dt):
            ap = nc.dram_tensor(name, list(shape), dt, kind="Internal").ap()
            if name in c.dbg_scr:
                twin = nc.dram_tensor("dbg_" + name, list(shape), dt,
                                      kind="ExternalOutput").ap()
                self.dbg_twins.append((ap, twin))
            return ap

        d = self.__dict__
        d["xb"] = din("xb", (c.S, c.D))
        d["xhalo"] = din("xhalo", (c.TH, c.D))
        d["meta"] = din("meta", (128, 8))
        d["iota0"] = din("iota0", (128, 1))
        d["ident"] = din("ident", (128, 128))
        d["ltri"] = din("ltri", (32, 32))
        d["w1T_t"] = din("w1T_t", (c.Dc, c.Rc, 128, 128))
        d["b1_pc"] = din("b1_pc", (128, c.Rc))
        d["convT_t"] = din("convT_t", (4, c.Rc, c.Rc, 128, 128))
        d["convb_pc"] = din("convb_pc", (128, c.Rc))
        d["waT_t"] = din("waT_t", (c.Rc, c.Rc, 128, 128))
        d["ba_pc"] = din("ba_pc", (128, c.Rc))
        d["wxT_t"] = din("wxT_t", (c.Rc, c.Rc, 128, 128))
        d["bx_pc"] = din("bx_pc", (128, c.Rc))
        d["lam_pc"] = din("lam_pc", (128, c.Rc))
        d["w2T"] = din("w2T", (c.R, c.D))
        d["b2v"] = din("b2v", (1, c.D))
        d["maskmy"] = din("maskmy", (c.OV, c.TOPK))
        d["cos_all"] = din("cos_all", (64, c.TOPK))
        d["sin_all"] = din("sin_all", (64, c.TOPK))
        d["cos_my"] = din("cos_my", (64, c.OV))
        d["sin_my"] = din("sin_my", (64, c.OV))
        d["router_v"] = din("router_v", (1, c.D))
        d["g1v"] = din("g1v", (1, c.D))
        d["g2v"] = din("g2v", (1, c.D))
        d["wq_t"] = din("wq_t", (c.Dc, c.Dc, 128, 128))
        d["wk_t"] = din("wk_t", (c.Dc, c.Dc, 128, 128))
        d["wv_m"] = din("wv_m", (c.D, c.D))
        d["wo_t"] = din("wo_t", (c.Dc, c.Dc, 128, 128))
        d["fw1_t"] = din("fw1_t", (c.Dc, c.Fc, 128, 128))
        d["fw3_t"] = din("fw3_t", (c.Dc, c.Fc, 128, 128))
        d["fw2_t"] = din("fw2_t", (c.Fc, c.Dc, 128, 128))
        d["out_d"] = nc.dram_tensor("out", [c.Tc, c.D], F32, kind="ExternalOutput").ap()
        d["strip"] = scr("strip", (c.THp, c.D), BF16)
        d["base_scr"] = scr("base_scr", (c.Tc, c.D), F32)
        d["lg_my"] = scr("lg_my", (1, c.Tc), F32)
        d["lg_pair"] = scr("lg_pair", (2, c.Tc), F32)
        d["selm_in"] = scr("selm_in", (c.TOPK, 2), F32)
        d["selm"] = scr("selm", (c.TOPK, 2), F32)
        d["carry_in"] = scr("carry_in", (128, c.Rc), F32)
        d["carry_pair"] = scr("carry_pair", (256, c.Rc), F32)
        d["h1_scr"] = scr("h1_scr", (c.Rc, 128, c.TH), BF16)
        d["h2_scr"] = scr("h2_scr", (c.R, c.Tc), BF16)
        d["p1x"] = scr("p1x", (128, c.D), F32)
        d["p2f"] = scr("p2f", (128, 6 * c.Tt + 16), F32)
        d["p2u"] = scr("p2u", (128, 2 * c.Tt), U32)
        d["p2s"] = scr("p2s", (64, 32), F32)
        d["p1xn"] = scr("p1xn", (128, c.D), F32)
        d["p1st"] = scr("p1st", (128, 12), F32)
        d["h2f_scr"] = scr("h2f_scr", (c.Rc, 128, c.Tc), F32)
        d["xw_pair"] = scr("xw_pair", (2 * c.OV, c.D), BF16)
        d["h_all_scr"] = scr("h_all_scr", (c.TOPK, c.D), BF16)
        d["h_my_scr"] = scr("h_my_scr", (c.OV, c.D), BF16)
        d["h2b_scr"] = scr("h2b_scr", (c.OV, c.D), BF16)
        d["o_tm_scr"] = scr("o_tm_scr", (c.OV, c.D), BF16)
        d["prod_scr"] = scr("prod_scr", (c.FFN, c.OV), BF16)
        d["xw_scr"] = scr("xw_scr", (c.OV, c.D), BF16)
        d["h3_scr"] = scr("h3_scr", (c.R, c.Tc), BF16)
        d["P_scr"] = scr("P_scr", (c.R, c.Tc), BF16)
        d["xf_scr"] = scr("xf_scr", (c.OV, c.D), BF16)
        d["xfall_scr"] = scr("xfall_scr", (c.TOPK, c.D), BF16)
        d["kro_scr"] = scr("kro_scr", (128, c.H * c.TOPK), BF16)
        d["qro_scr"] = scr("qro_scr", (128, c.H * c.OV), BF16)
        d["v_scr"] = scr("v_scr", (c.TOPK, c.D), BF16)
        d["xa_scr"] = scr("xa_scr", (c.OV, c.D), BF16)
        d["dblk_scr"] = scr("dblk_scr", (c.OV, c.D), BF16)
        d["o_fm_scr"] = scr("o_fm_scr", (128, c.Dc * c.OV), BF16)
        sb = lambda n, p, f, dt: es.enter_context(nc.sbuf_tensor(n, [p, f], dt))
        d["cst"] = sb("cst", 128, len(self.consts), F32)
        d["meta_sb"] = sb("meta_sb", 128, 8, F32)
        d["iota0_sb"] = sb("iota0_sb", 128, 1, F32)
        d["ident_f"] = sb("ident_f", 128, 128, F32)
        d["ident_bf"] = sb("ident_bf", 128, 128, BF16)
        d["ltri_sb"] = sb("ltri_sb", 32, 32, F32)
        d["ones_bf"] = sb("ones_bf", 128, 1, BF16)
        d["ones_r128_bf"] = sb("ones_r128_bf", 1, 128, BF16)
        d["ones_col"] = sb("ones_col", 128, 1, F32)
        d["ones_row"] = sb("ones_row", 1, 64, F32)
        d["iota1g"] = sb("iota1g", 64, 1, F32)
        d["b1_sb"] = sb("b1_sb", 128, c.Rc, F32)
        d["convb_sb"] = sb("convb_sb", 128, c.Rc, F32)
        d["ba_sb"] = sb("ba_sb", 128, c.Rc, F32)
        d["bx_sb"] = sb("bx_sb", 128, c.Rc, F32)
        d["msp8_sb"] = sb("msp8_sb", 128, c.Rc, F32)
        d["b2_bf"] = sb("b2_bf", 1, c.D, BF16)
        d["sc1"] = sb("sc1", 1, 16, F32)
        d["drain_sb"] = sb("drain_sb", 1, 8, F32)
        d["scB"] = sb("scB", 32, 8, F32)
        d["st1"] = sb("st1", 128, 12, F32)
        d["lsbuf"] = sb("lsbuf", 64, 4, F32)
        d["cnt64"] = sb("cnt64", 64, 1, F32)
        d["flag64"] = sb("flag64", 64, 1, F32)
        d["tau64"] = sb("tau64", 64, 1, F32)
        d["zblk"] = sb("zblk", 32, 128, F32)
        d["zrow512"] = sb("zrow512", 1, 512, F32)
        d["idxg_pc"] = sb("idxg_pc", 128, c.Tt, F32)
        d["rw_pc"] = sb("rw_pc", 128, c.Tt, F32)
        d["m_pc"] = sb("m_pc", 128, c.Tt, F32)
        d["rank_pc"] = sb("rank_pc", 128, c.Tt, F32)
        d["sidx_u"] = sb("sidx_u", 128, c.Tt, U32)
        d["idxg_u"] = sb("idxg_u", 128, c.Tt, U32)
        d["pay_sb"] = sb("pay_sb", 128, 2 * c.Tt, F32)
        d["zpad_sb"] = sb("zpad_sb", 128, 2 * c.Kc, F32)
        d["selidx_u"] = sb("selidx_u", 128, c.OVc, U32)
        d["allid_u"] = sb("allid_u", 128, c.Kc, U32)
        d["myid_u"] = sb("myid_u", 128, c.OVc, U32)
        d["selmeta_my"] = sb("selmeta_my", 128, c.OVc * 2, F32)
        d["ids_all"] = sb("ids_all", 128, c.Kc * 2, F32)
        d["idxf"] = sb("idxf", 128, max(c.Tt, c.OVc, c.Kc), F32)
        d["carry_sb"] = sb("carry_sb", 128, c.Rc, F32)
        d["carryP_sb"] = sb("carryP_sb", 128, c.Rc, F32)
        d["rc_sb"] = sb("rc_sb", 128, c.Rc, F32)
        d["zacc"] = sb("zacc", c.H, c.OV, F32)
        d["zoh"] = sb("zoh", 128, c.H * c.H, BF16)
        d["zrec_sb"] = sb("zrec_sb", 128, c.OVc * c.H, F32)
        d["ones512"] = sb("ones512", 128, 512, F32)
        d["ropetmp"] = sb("ropetmp", 128, 1024, F32)
        d["ps"] = [es.enter_context(nc.psum_tensor(f"ps{i}", [128, 512], F32))
                   for i in range(8)]
        names = ("qH", "qS", "qG", "cT", "cV", "cS", "cG", "cC", "cX", "cY")
        for s in names:
            d[s] = Sem(es, nc, s)
        d["sems"] = [d[s] for s in names]
        d["fence"] = {}
        d["block"] = es.enter_context(nc.Block())
        for v in self.consts:
            nc.const_aps.register(F32, v, self.C(v))
        d["pairs"] = [[2 * i, 2 * i + 1] for i in range(c.ncores // 2)]
        d["M_ODD"] = self.meta_sb[:, 0:1]
        d["M_QB"] = self.meta_sb[:, 1:2]
        d["M_TB"] = self.meta_sb[:, 2:3]
        d["M_NOTODD"] = self.meta_sb[:, 3:4]

    def p0(self):
        c = self.c

        def p0_sync(sync):
            for dst, src in [(self.meta_sb[:, :], self.meta[:, :]),
                             (self.iota0_sb[:, :], self.iota0[:, :]),
                             (self.ident_f[:, :], self.ident[:, :]),
                             (self.ltri_sb[:, :], self.ltri[:, :]),
                             (self.b1_sb[:, :], self.b1_pc[:, :]),
                             (self.convb_sb[:, :], self.convb_pc[:, :]),
                             (self.ba_sb[:, :], self.ba_pc[:, :]),
                             (self.bx_sb[:, :], self.bx_pc[:, :]),
                             (self.msp8_sb[:, :], self.lam_pc[:, :])]:
                self.qH.tag(sync.dma_start(out=dst, in_=src), 16)
                sync.wait_ge(self.qH.h, self.qH.n)
        self.block.sync(p0_sync)
        self.P0_QH = self.qH.n

        def p0_gpsimd(g):
            self.qG.tag(g.dma_start(out=self.ident_bf[:, :], in_=self.ident[:, :]), 16)
            self.qG.tag(g.dma_start(out=self.b2_bf[0:1, :], in_=self.b2v[:, :]), 16)
        self.block.gpsimd(p0_gpsimd)

        def p0_vector(v):
            for val in self.consts:
                v.memset(self.C(val), val)
            for t in (self.ones_bf, self.ones_col, self.ones_row,
                      self.ones_r128_bf):
                v.memset(t[:, :], 1.0)
            for t in (self.zblk, self.zrow512, self.zpad_sb):
                v.memset(t[:, :], 0.0)
            v.memset(self.ones512[:, :], 1.0)
            v.memset(self.carry_sb[:, :], 0.0)
            v.memset(self.carryP_sb[:, :], 1.0)
            v.memset(self.zoh[:, :], 0.0)
            v.memset(self.drain_sb[:, :], 0.0)
            for h in range(self.c.H):
                v.memset(self.zoh[:, h * self.c.H + h:h * self.c.H + h + 1], 1.0)
            v.wait_ge(self.qH.h, 16 * 2)
            self.cV.tag(v.tensor_scalar(self.iota1g[0:64, 0:1],
                                        self.iota0_sb[0:64, 0:1],
                                        self.cst[0:64, self.cmap[1.0]:self.cmap[1.0] + 1],
                                        None, ALU.add))
        self.block.vector(p0_vector)

        def p0_scalar(s):
            s.wait_ge(self.qH.h, self.P0_QH)
            s.activation(self.msp8_sb[:, :], self.msp8_sb[:, :], AF.Exp)
            s.activation(self.msp8_sb[:, :], self.msp8_sb[:, :], AF.Ln, bias=self.C(1.0))
            self.cS.tag(s.activation(self.msp8_sb[:, :], self.msp8_sb[:, :],
                                     AF.Copy, scale=self.C(-8.0)))
        self.block.scalar(p0_scalar)
        self.fence = {s: s.n for s in self.sems}

    def p1(self):
        c = self.c
        self.phase_open()
        x_t = [self.sb(f"x_t{i}", 128, c.D, F32) for i in range(2)]
        xn_t = [self.sb(f"xn_t{i}", 128, c.D, F32) for i in range(2)]
        bs_t = [self.sb(f"bs_t{i}", 128, c.D, F32) for i in range(2)]
        xnb_t = [self.sb(f"xnb_t{i}", 128, c.D, BF16) for i in range(2)]
        sq_t = self.sb("sq_t", 128, c.D, F32)
        vdump = self.sb("vdump", 128, c.D, F32)
        router_rep = self.sb("router_rep", 128, c.D, F32)
        QH0, CS0, CV0, QG0 = self.qH.n, self.cS.n, self.cV.n, self.qG.n
        NT = c.Tt

        def p1_sync(sync):
            sync.wait_ge(self.qH.h, self.P0_QH)
            self.qH.tag(sync.dma_start(out=router_rep[0:1, :], in_=self.router_v[:, :]), 16)
            sync.wait_ge(self.qH.h, self.qH.n)
            for t in range(NT):
                b = t % 2
                if t >= 1:
                    sync.wait_ge(self.cV.h, CV0 + 3 * t)
                self.qH.tag(sync.dma_start(out=x_t[b][:, :],
                                           in_=self.xhalo[3 + 128 * t:131 + 128 * t, :]), 16)
                sync.wait_ge(self.qH.h, self.qH.n)
            sync.wait_ge(self.cV.h, CV0 + 3 * (NT - 1))
            self.qH.tag(sync.dma_start(out=x_t[0][0:3, :], in_=self.xhalo[0:3, :]), 16)
            self.qH.tag(sync.dma_start(out=x_t[0][3:6, :],
                                       in_=self.xhalo[c.TH - 3:c.TH, :]), 16)
        self.block.sync(p1_sync)

        def p1_qs(scalar):
            scalar.wait_ge(self.qH.h, QH0 + 16)
            n = 1
            while n < 128:
                k = min(n, 128 - n)
                self.qS.tag(scalar.dma_start(out=router_rep[n:n + k, :],
                                             in_=router_rep[0:n, :][0:k, :]), 16)
                scalar.wait_ge(self.qS.h, self.qS.n)
                n += k
        self.block.scalar(p1_qs)
        QS_ROUTER = self.qS.n

        def ln_chain(s, xin, xout, rows, cb=4):
            st, sq = self.st1, sq_t
            s.activation(sq[0:rows, :], xin, AF.Copy, accum_out=st[0:rows, 0:1])
            s.activation(sq[0:rows, :], xin, AF.Square, accum_out=st[0:rows, 1:2])
            s.mul(st[0:rows, 0:1], st[0:rows, 0:1], self.inv_d)
            s.mul(st[0:rows, 2:3], st[0:rows, 1:2], self.inv_d)
            s.activation(st[0:rows, 3:4], st[0:rows, 0:1], AF.Square)
            s.mul(st[0:rows, 3:4], st[0:rows, 3:4], -1.0)
            s.add(st[0:rows, 2:3], st[0:rows, 2:3], self.Cp(1e-5, rows))
            s.add(st[0:rows, 2:3], st[0:rows, 2:3], st[0:rows, 3:4])
            s.mul(st[0:rows, 3:4], st[0:rows, 0:1], -1.0)
            s.activation(st[0:rows, 1:2], st[0:rows, 2:3], AF.Ln)
            s.activation(st[0:rows, cb:cb + 1], st[0:rows, 1:2], AF.Exp,
                         scale=self.Cp(-0.5, rows))
            for it in range(2):
                if it == 1:
                    s.mul(st[0:rows, 1:2], st[0:rows, 2:3], 1.0)
                s.activation(st[0:rows, 1:2], st[0:rows, cb:cb + 1], AF.Square)
                s.mul(st[0:rows, 1:2], st[0:rows, 1:2], st[0:rows, 2:3])
                s.mul(st[0:rows, 1:2], st[0:rows, 1:2], -0.5)
                s.add(st[0:rows, 1:2], st[0:rows, 1:2], self.Cp(1.5, rows))
                s.mul(st[0:rows, cb:cb + 1], st[0:rows, 1:2],
                      st[0:rows, cb:cb + 1])
            self.cS.tag(s.mul(st[0:rows, cb + 1:cb + 2], st[0:rows, cb:cb + 1],
                              st[0:rows, 3:4]))

        def p1_scalar(s):
            for t in range(NT):
                b = t % 2
                s.wait_ge(self.qH.h, QH0 + 16 * (t + 2))
                if t >= 1:
                    s.wait_ge(self.cV.h, CV0 + 3 * t)
                ln_chain(s, x_t[b][:, :], xn_t[b][:, :], 128, cb=4 + 4 * (t % 2))
            s.wait_ge(self.qH.h, QH0 + 16 * (NT + 3))
            s.wait_ge(self.cV.h, CV0 + 3 * (NT - 1))
            if not c.dbg_notail:
                ln_chain(s, x_t[0][0:6, :], xn_t[0][0:6, :], 6,
                         cb=4 + 4 * (NT % 2))
        self.block.scalar(p1_scalar)

        def p1_vector(v):
            v.wait_ge(self.qS.h, QS_ROUTER)
            for t in range(NT):
                b = t % 2
                v.wait_ge(self.cS.h, CS0 + t + 1)
                v.wait_ge(self.qH.h, QH0 + 16 * (t + 2))
                if t >= 2:
                    v.wait_ge(self.qG.h, QG0 + 48 * (t - 1))
                col = 6 + (t % 2)
                cb = 4 + 4 * (t % 2)
                v.tensor_scalar(xn_t[b][:, :], x_t[b][:, :], self.st1[:, cb:cb + 1],
                                self.st1[:, cb + 1:cb + 2], ALU.mult, ALU.add)
                v.scalar_tensor_tensor(vdump[:, :], xn_t[b][:, :], self.C(1.0),
                                       router_rep[:, :], ALU.mult, ALU.mult,
                                       accum_out=self.st1[:, col:col + 1])
                v.tensor_tensor(bs_t[b][:, :], x_t[b][:, :], xn_t[b][:, :], ALU.add)
                self.cV.tag(v.tensor_copy(xnb_t[b][:, :], xn_t[b][:, :]), 3)
            if c.dbg_notail:
                return
            v.wait_ge(self.cS.h, CS0 + NT + 1)
            cbt = 4 + 4 * (NT % 2)
            v.tensor_scalar(xn_t[0][0:6, :], x_t[0][0:6, :], self.st1[0:6, cbt:cbt + 1],
                            self.st1[0:6, cbt + 1:cbt + 2], ALU.mult, ALU.add)
            self.cV.tag(v.tensor_copy(xnb_t[0][0:6, :], xn_t[0][0:6, :]), 1)
        self.block.vector(p1_vector)

        def p1_gpsimd(g):
            for t in range(NT):
                b = t % 2
                col = 6 + (t % 2)
                g.wait_ge(self.cV.h, CV0 + 3 * (t + 1))
                self.qG.tag(g.dma_start(out=self.strip[3 + 128 * t:131 + 128 * t, :],
                                        in_=xnb_t[b][:, :]), 16)
                self.qG.tag(g.dma_start(out=self.base_scr[128 * t:128 * (t + 1), :],
                                        in_=bs_t[b][:, :]), 16)
                dst = self.lg_my[0:1, 128 * t:128 * (t + 1)].rearrange(
                    "one (p w) -> (one p) w", p=128)
                self.qG.tag(g.dma_start(out=dst, in_=self.st1[:, col:col + 1]), 16)
            if not c.dbg_notail:
                g.wait_ge(self.cV.h, CV0 + 3 * NT + 1)
                self.qG.tag(g.dma_start(out=self.strip[0:3, :],
                                        in_=xnb_t[0][0:3, :]), 16)
                self.qG.tag(g.dma_start(out=self.strip[c.TH - 3:c.TH, :],
                                        in_=xnb_t[0][3:6, :]), 16)
            g.wait_ge(self.qG.h, self.qG.n)
            g.memset(xnb_t[1][:, :], 0.0)
            if c.THp > c.TH:
                self.qG.tag(g.dma_start(out=self.strip[c.TH:c.THp, :],
                                        in_=xnb_t[1][0:c.THp - c.TH, :]), 16)
            g.wait_ge(self.qG.h, self.qG.n)
            if "p1x" in c.dbg_scr:
                self.qG.tag(g.dma_start(out=self.p1x[:, :], in_=x_t[1][:, :]), 16)
                self.qG.tag(g.dma_start(out=self.p1xn[:, :], in_=xn_t[1][:, :]), 16)
                self.qG.tag(g.dma_start(out=self.p1st[:, :], in_=self.st1[:, :]), 16)
                g.wait_ge(self.qG.h, self.qG.n)
            self.cC.tag(g.collective_compute("AllGather", ALU.bypass,
                                             replica_groups=self.pairs,
                                             ins=[self.lg_my[:, :]],
                                             outs=[self.lg_pair[:, :]]))
        self.block.gpsimd(p1_gpsimd)
        self.phase_close()

    def p2(self):
        c = self.c
        self.phase_open()
        xn_fm = self.sb("xn_fm", 128, c.Dc * c.THp, BF16)
        w1T_b = self.sb("w1T_b", 128, 2 * c.Dc * 128, BF16)
        stg = [self.sb(f"h1stg{i}", 128, c.TH, BF16) for i in range(2)]
        lgrep = self.sb("lgrep", 1, c.S, F32)
        jkrep = self.sb("jkrep", 1, c.S, F32)
        n2 = 2 * c.Tt
        assert n2 <= 32
        lgall_b = self.sb("lgall_b", n2, 128, F32)
        lgmy_b = self.sb("lgmy_b", c.Tt, 128, F32)
        eall_b = self.sb("eall_b", n2, 128, F32)
        emy_b = self.sb("emy_b", c.Tt, 128, F32)
        gt_g = self.sb("gt_g", n2, 128, F32)
        eq_g = self.sb("eq_g", n2, 128, F32)
        wk_g = self.sb("wk_g", n2, 128, F32)
        gts = self.sb("gts", n2, 1, F32)
        gts2 = self.sb("gts2", n2, 1, F32)
        zs = self.sb("zs", n2, 1, F32)
        cq = self.sb("cq", n2, 1, F32)
        gt_m = self.sb("gt_m", c.Tt, 128, F32)
        eq_m = self.sb("eq_m", c.Tt, 128, F32)
        m_blk = self.sb("m_blk", c.Tt, 128, F32)
        ceq_m = self.sb("ceq_m", c.Tt, 128, F32)
        rk_m = self.sb("rk_m", c.Tt, 128, F32)
        rw_blk = self.sb("rw_blk", c.Tt, 128, F32)
        ix_blk = self.sb("ix_blk", c.Tt, 128, F32)
        t_blk = self.sb("t_blk", c.Tt, 128, F32)
        QS0, QG0, CT0, CS0 = self.qS.n, self.qG.n, self.cT.n, self.cS.n
        CX0, CY0 = self.cX.n, self.cY.n
        assert CX0 == 0 and CY0 == 0
        self.qGs = Sem(self.es, self.nc, "qGs")
        self.sems.append(self.qGs)
        qGs = self.qGs
        segs = [(512 * i, 512) for i in range(c.Tc // 512)]
        NS1 = len(segs) + 1
        NTAIL = c.TH - c.Tc
        fence = dict(self.fence)

        def qs_sec(scalar):
            self.fwait(scalar)
            for dch in range(c.Dc):
                self.qS.tag(scalar.dma_start(
                    out=xn_fm[:, dch * c.THp:(dch + 1) * c.THp],
                    in_=self.strip[0:c.THp, 128 * dch:128 * (dch + 1)],
                    transpose=True), 16)
            scalar.wait_ge(self.qS.h, self.qS.n)
            scalar.wait_ge(self.cC.h, fence[self.cC])
            self.qS.tag(scalar.dma_start(
                out=lgrep[0:1, :],
                in_=self.lg_pair[:, :].rearrange("two t -> (two t)").rearrange(
                    "(one s) -> one s", one=1)), 16)
            scalar.wait_ge(self.qS.h, self.qS.n)
            self.qS.tag(scalar.dma_start(
                out=lgall_b[:, :],
                in_=self.lg_pair[:, :].rearrange("two (a b) -> (two a) b", a=c.Tt)), 16)
            self.qS.tag(scalar.dma_start(
                out=lgmy_b[:, :],
                in_=self.lg_my[:, :].rearrange("one (a b) -> (one a) b", a=c.Tt)), 16)
        self.block.scalar(qs_sec)
        self.P2_QS = self.qS.n
        QS_XN = QS0 + 16 * c.Dc
        NSR = NS1 + 1

        def g_w1(g):
            self.fwait(g)
            for r in range(c.Rc + 2):
                if r < c.Rc:
                    par = r % 2
                    off = par * c.Dc * 128
                    if r >= 2:
                        g.wait_ge(self.cT.h, CT0 + NS1 * (r - 1))
                    kh = max(1, c.Dc // 2)
                    for half in range(c.Dc // kh):
                        srcw = self.w1T_t[half * kh:(half + 1) * kh, r, :, :]
                        dstw = w1T_b[:, off + half * kh * 128:
                                     off + (half + 1) * kh * 128]
                        self.qG.tag(g.dma_start(
                            out=dstw.rearrange("p (k w) -> p k w", k=kh),
                            in_=srcw.rearrange("k p w -> p k w")), 16)
                    g.wait_ge(self.qG.h, self.qG.n)
                if r >= 2:
                    j = r - 2
                    g.wait_ge(self.cS.h, CS0 + NSR * (j + 1))
                    qGs.tag(g.dma_start(
                        out=self.h1_scr[j, :, :], in_=stg[j % 2][:, 0:c.TH]), 16)
        self.block.gpsimd(g_w1)
        NW1 = 2 if c.Dc >= 2 else 1

        def t_sec(t):
            self.fwait(t)
            t.wait_ge(self.qS.h, QS_XN)
            for r in range(c.Rc):
                woff = (r % 2) * c.Dc * 128
                t.wait_ge(self.qG.h, QG0 + 16 * NW1 * (r + 1))
                for si, (s0, w) in enumerate(segs):
                    if r >= 1:
                        t.wait_ge(self.cS.h, CS0 + NSR * (r - 1) + si + 1)
                    for k in range(c.Dc):
                        ins = t.matmul(self.ps[si][:, 0:w],
                                       w1T_b[:, woff + k * 128:woff + (k + 1) * 128],
                                       xn_fm[:, k * c.THp + s0:k * c.THp + s0 + w],
                                       start=(k == 0), stop=(k == c.Dc - 1))
                    self.cT.tag(ins)
                if r >= 1:
                    t.wait_ge(self.cS.h, CS0 + NSR * (r - 1) + NS1)
                for k in range(c.Dc):
                    ins = t.matmul(self.ps[4][:, 0:NTAIL],
                                   w1T_b[:, woff + k * 128:woff + (k + 1) * 128],
                                   xn_fm[:, k * c.THp + c.Tc:k * c.THp + c.TH],
                                   start=(k == 0), stop=(k == c.Dc - 1))
                self.cT.tag(ins)
            self._post_T(t)
        self.t_sec_p2 = t_sec

        def s_evac(s):
            for r in range(c.Rc):
                b = r % 2
                if r >= 2:
                    s.wait_ge(qGs.h, 16 * (r - 1))
                for si, (s0, w) in enumerate(segs):
                    s.wait_ge(self.cT.h, CT0 + NS1 * r + si + 1)
                    self.cS.tag(s.activation(stg[b][:, s0:s0 + w], self.ps[si][:, 0:w],
                                             AF.Silu, bias=self.b1_sb[:, r:r + 1]))
                s.wait_ge(self.cT.h, CT0 + NS1 * r + NS1)
                self.cS.tag(s.activation(stg[b][:, c.Tc:c.TH], self.ps[4][:, 0:NTAIL],
                                         AF.Silu, bias=self.b1_sb[:, r:r + 1]))
                s.mul(stg[b][:, 0:3], stg[b][:, 0:3], self.M_ODD)
                self.cS.tag(s.mul(stg[b][:, c.TH - 3:c.TH], stg[b][:, c.TH - 3:c.TH],
                                  self.M_NOTODD))
            s.wait_ge(self.cY.h, CY0 + 1)
            self.cS.tag(s.activation(eall_b[:, :], lgall_b[:, :], AF.Exp,
                                     bias=self.Cp(CRW, n2)))
            self.cS.tag(s.activation(emy_b[:, :], lgmy_b[:, :], AF.Exp,
                                     bias=self.Cp(CRW, c.Tt)))
        self.s_evac_p2 = s_evac
        self.CS_EVAC_END = CS0 + NSR * c.Rc
        self.CS_EXP_ALL = self.CS_EVAC_END + 1
        self.CS_EXP_MY = self.CS_EVAC_END + 2

        def v_sec(v):
            sc, C = self.sc1, self.C
            v.wait_ge(self.qS.h, self.P2_QS)
            v.reduce_max(sc[0:1, 1:2], lgrep[0:1, :], axis=AXX)
            v.tensor_reduce(sc[0:1, 0:1], lgrep[0:1, :], axis=AXX, op=ALU.min)
            self.sp(v)
            v.tensor_scalar(sc[0:1, 0:1], sc[0:1, 0:1], self.Cp(1.0, 1), None,
                            ALU.subtract)
            self.sp(v)
            self.sp(v)
            for k in range(45):
                v.tensor_tensor(sc[0:1, 2:3], sc[0:1, 0:1], sc[0:1, 1:2], ALU.add)
                self.sp(v)
                v.tensor_single_scalar(sc[0:1, 2:3], sc[0:1, 2:3], self.Cp(0.5, 1),
                                       ALU.mult)
                self.sp(v)
                self.sp(v)
                v.tensor_scalar(jkrep[0:1, :], lgrep[0:1, :], sc[0:1, 2:3], None,
                                ALU.is_gt, ALU.add, accum_out=sc[0:1, 3:4])
                self.sp(v)
                self.sp(v)
                v.tensor_single_scalar(sc[0:1, 15:16], sc[0:1, 3:4],
                                       self.Cp(float(c.TOPK), 1), ALU.is_ge)
                self.sp(v)
                self.sp(v)
                v.tensor_tensor(sc[0:1, 14:15], sc[0:1, 2:3], sc[0:1, 0:1],
                                ALU.subtract)
                self.sp(v)
                v.tensor_tensor(sc[0:1, 14:15], sc[0:1, 14:15], sc[0:1, 15:16],
                                ALU.mult)
                self.sp(v)
                self.sp(v)
                v.tensor_tensor(sc[0:1, 0:1], sc[0:1, 14:15], sc[0:1, 0:1], ALU.add)
                v.tensor_scalar(sc[0:1, 15:16], sc[0:1, 15:16], self.Cp(-1.0, 1),
                                self.Cp(1.0, 1), ALU.mult, ALU.add)
                self.sp(v)
                v.tensor_tensor(sc[0:1, 14:15], sc[0:1, 2:3], sc[0:1, 1:2],
                                ALU.subtract)
                self.sp(v)
                v.tensor_tensor(sc[0:1, 14:15], sc[0:1, 14:15], sc[0:1, 15:16],
                                ALU.mult)
                self.sp(v)
                self.sp(v)
                v.tensor_tensor(sc[0:1, 1:2], sc[0:1, 14:15], sc[0:1, 1:2], ALU.add)
                self.sp(v)
            self.tagY(v)
            v.tensor_copy(self.drain_sb[0:1, 2:5], sc[0:1, 0:3])
            v.wait_ge(self.cX.h, CX0 + 1)
            v.tensor_copy(self.scB[0:32, 0:2], self.ps[6][0:32, 0:2])
            v.tensor_copy(self.drain_sb[0:1, 6:7], self.drain_sb[0:1, 0:1])
            v.tensor_scalar(gt_g[:, :], lgall_b[:, :], self.scB[0:n2, 1:2], None,
                            ALU.is_gt, ALU.add, accum_out=gts[:, 0:1])
            v.tensor_scalar(eq_g[:, :], lgall_b[:, :], self.scB[0:n2, 0:1], None,
                            ALU.is_gt, ALU.add, accum_out=gts2[:, 0:1])
            v.tensor_tensor(eq_g[:, :], eq_g[:, :], gt_g[:, :], ALU.subtract)
            v.tensor_tensor(gts2[:, 0:1], gts2[:, 0:1], gts[:, 0:1], ALU.subtract)
            v.tensor_tensor_scan(wk_g[:, :], eq_g[:, :], self.zblk[0:n2, :],
                                 0.0, ALU.add, ALU.add)
            self.tagY(v)
            v.wait_ge(self.cX.h, CX0 + 2)
            v.tensor_copy(cq[:, 0:1], self.ps[6][0:n2, 2:3])
            v.tensor_copy(sc[0:1, 4:5], self.ps[7][0:1, 0:1])
            v.tensor_copy(sc[0:1, 6:7], self.ps[7][0:1, 1:2])
            v.tensor_copy(sc[0:1, 7:8], self.ps[7][0:1, 2:3])
            self.sp(v)
            v.tensor_scalar(sc[0:1, 5:6], sc[0:1, 4:5], self.Cp(float(c.TOPK), 1),
                            self.Cp(-1.0, 1), ALU.subtract, ALU.mult)
            self.tagY(v)
            v.wait_ge(self.cX.h, CX0 + 3)
            v.tensor_copy(self.scB[0:32, 2:3], self.ps[6][0:32, 3:4])
            v.tensor_scalar(wk_g[:, :], wk_g[:, :], cq[:, 0:1], None, ALU.add)
            v.tensor_scalar(wk_g[:, :], wk_g[:, :], self.scB[0:n2, 2:3], None,
                            ALU.is_le)
            v.tensor_tensor(wk_g[:, :], wk_g[:, :], eq_g[:, :], ALU.mult)
            v.tensor_tensor(gt_g[:, :], wk_g[:, :], gt_g[:, :], ALU.add)
            v.tensor_copy(self.drain_sb[0:1, 6:7], self.drain_sb[0:1, 0:1])
            v.wait_ge(self.cS.h, self.CS_EXP_ALL)
            v.scalar_tensor_tensor(eall_b[:, :], eall_b[:, :],
                                   self.Cp(1.0, n2),
                                   gt_g[:, :], ALU.mult, ALU.mult,
                                   accum_out=zs[:, 0:1])
            self.tagY(v)
            v.wait_ge(self.cX.h, CX0 + 4)
            v.tensor_copy(sc[0:1, 8:9], self.ps[7][0:1, 3:4])
            self.sp(v)
            v.reciprocal(sc[0:1, 9:10], sc[0:1, 8:9])
            v.tensor_tensor(sc[0:1, 14:15], sc[0:1, 7:8], sc[0:1, 5:6], ALU.min)
            self.sp(v)
            v.tensor_tensor(sc[0:1, 10:11], sc[0:1, 14:15], sc[0:1, 6:7], ALU.add)
            odd = self.meta_sb[0:1, 0:1]
            qb = self.meta_sb[0:1, 1:2]
            v.tensor_tensor(sc[0:1, 11:12], odd, sc[0:1, 7:8], ALU.mult)
            self.sp(v)
            v.tensor_tensor(sc[0:1, 12:13], odd, sc[0:1, 10:11], ALU.mult)
            self.sp(v)
            v.tensor_scalar(sc[0:1, 12:13], sc[0:1, 12:13], self.Cp(-1.0, 1), None, ALU.add)
            self.sp(v)
            v.tensor_tensor(sc[0:1, 13:14], sc[0:1, 12:13], qb, ALU.subtract)
            v.tensor_copy(sc[0:1, 14:15], sc[0:1, 9:10])
            self.tagY(v)
            v.wait_ge(self.cX.h, CX0 + 5)
            v.tensor_copy(self.scB[0:32, 3:7], self.ps[6][0:32, 4:8])
            Tt = c.Tt
            v.tensor_scalar(gt_m[:, :], lgmy_b[:, :], self.scB[0:Tt, 1:2], None,
                            ALU.is_gt)
            v.tensor_scalar(eq_m[:, :], lgmy_b[:, :], self.scB[0:Tt, 0:1], None,
                            ALU.is_gt)
            v.tensor_tensor(eq_m[:, :], eq_m[:, :], gt_m[:, :], ALU.subtract)
            v.tensor_copy(self.drain_sb[0:1, 6:7], self.drain_sb[0:1, 0:1])
            v.tensor_tensor_scan(ceq_m[:, :], eq_m[:, :],
                                 self.zblk[0:Tt, :], 0.0, ALU.add, ALU.add)
            self.tagY(v)
            v.wait_ge(self.cX.h, CX0 + 6)
            v.tensor_copy(cq[0:Tt, 0:1], self.ps[6][0:Tt, 2:3])
            v.tensor_copy(self.drain_sb[0:1, 6:7], self.drain_sb[0:1, 0:1])
            v.tensor_scalar(ceq_m[:, :], ceq_m[:, :], cq[0:Tt, 0:1],
                            self.scB[0:Tt, 3:4], ALU.add, ALU.add)
            v.tensor_scalar(ceq_m[:, :], ceq_m[:, :], self.scB[0:Tt, 2:3], None,
                            ALU.is_le)
            v.tensor_tensor(ceq_m[:, :], ceq_m[:, :], eq_m[:, :], ALU.mult)
            v.tensor_tensor(m_blk[:, :], ceq_m[:, :], gt_m[:, :], ALU.add)
            v.tensor_copy(self.drain_sb[0:1, 6:7], self.drain_sb[0:1, 0:1])
            v.tensor_tensor_scan(rk_m[:, :], m_blk[:, :],
                                 self.zblk[0:Tt, :], 0.0, ALU.add, ALU.add)
            self.tagY(v)
            v.wait_ge(self.cX.h, CX0 + 7)
            v.tensor_copy(cq[0:Tt, 0:1], self.ps[6][0:Tt, 2:3])
            v.tensor_copy(self.drain_sb[0:1, 6:7], self.drain_sb[0:1, 0:1])
            v.tensor_scalar(ix_blk[:, :], rk_m[:, :], cq[0:Tt, 0:1],
                            self.scB[0:Tt, 5:6], ALU.add, ALU.add)
            v.tensor_scalar(rk_m[:, :], rk_m[:, :], cq[0:Tt, 0:1],
                            self.scB[0:Tt, 4:5], ALU.add, ALU.add)
            v.tensor_tensor(ix_blk[:, :], ix_blk[:, :], m_blk[:, :], ALU.mult)
            v.tensor_scalar(t_blk[:, :], m_blk[:, :], self.Cp(-1e6, Tt),
                            self.Cp(1e6, Tt), ALU.mult, ALU.add)
            v.tensor_tensor(ix_blk[:, :], t_blk[:, :], ix_blk[:, :], ALU.add)
            v.wait_ge(self.cS.h, self.CS_EXP_MY)
            v.tensor_tensor(rw_blk[:, :], emy_b[:, :], m_blk[:, :], ALU.mult)
            v.tensor_scalar(rw_blk[:, :], rw_blk[:, :],
                            self.scB[0:Tt, 6:7], None, ALU.mult)
            self.tagY(v)
            srcs = [(ix_blk, self.idxg_pc), (rw_blk, self.rw_pc),
                    (m_blk, self.m_pc), (rk_m, self.rank_pc)]
            for j in range(4):
                v.wait_ge(self.cT.h, self.CT_TP0 + j + 1)
                bank = [5, 6, 7, 4][j]
                v.tensor_copy(srcs[j][1][:, 0:Tt], self.ps[bank][:, 0:Tt])
            for t in range(Tt):
                v.tensor_scalar(self.idxf[:, t:t + 1], self.iota0_sb[:, 0:1],
                                self.M_TB, C(float(128 * t)), ALU.add, ALU.add)
            for i in range(c.OVc):
                v.tensor_scalar(self.st1[:, 0:1], self.iota0_sb[:, 0:1], self.M_QB,
                                C(float(128 * i)), ALU.add, ALU.add)
                v.tensor_copy(self.selidx_u[:, i:i + 1], self.st1[:, 0:1])
            for t in range(Tt):
                v.tensor_scalar(self.st1[:, 0:1], self.m_pc[:, t:t + 1], C(-1.0),
                                C(1.0), ALU.mult, ALU.add)
                v.tensor_scalar(self.st1[:, 1:2], self.idxf[:, t:t + 1],
                                C(float(c.TOPK)), None, ALU.add)
                v.tensor_tensor(self.st1[:, 1:2], self.st1[:, 1:2], self.st1[:, 0:1],
                                ALU.mult)
                v.tensor_tensor(self.st1[:, 2:3], self.rank_pc[:, t:t + 1],
                                self.m_pc[:, t:t + 1], ALU.mult)
                v.tensor_tensor(self.st1[:, 2:3], self.st1[:, 2:3], self.st1[:, 1:2],
                                ALU.add)
                v.tensor_copy(self.sidx_u[:, t:t + 1], self.st1[:, 2:3])
                v.tensor_copy(self.pay_sb[:, 2 * t:2 * t + 1], self.idxf[:, t:t + 1])
                v.tensor_copy(self.pay_sb[:, 2 * t + 1:2 * t + 2],
                              self.rw_pc[:, t:t + 1])
            self.cV.tag(v.tensor_copy(self.idxg_u[:, 0:Tt], self.idxg_pc[:, 0:Tt]))
            self.CV_PAY = self.cV.n
        self.v_sec_p2 = v_sec

        self.wk_srcs = dict(wk_g=wk_g, gts=gts, gts2=gts2, zs=zs, ceq_m=ceq_m,
                            m_scan=rk_m, ix_blk=ix_blk, rw_blk=rw_blk,
                            m_blk=m_blk, rk_scan=rk_m)
        self.CT_TP0 = CT0 + NS1 * c.Rc
        self.block.tensor(t_sec)
        self.block.scalar(s_evac)
        self.block.vector(v_sec)
        self.P2_QGS = qGs.n

        def g_scatter(g):
            g.wait_ge(self.cV.h, self.CV_PAY)
            self.qG.tag(g.dma_start(
                out=self.selm_in[:, :].rearrange("(a b) two -> a (b two)", a=128),
                in_=self.zpad_sb[:, 0:2 * c.Kc]), 16)
            g.wait_ge(self.qG.h, self.qG.n)
            for t in range(c.Tt):
                self.qG.tag(g.indirect_dma_start(
                    out=self.selm_in[:, :],
                    out_offset=IndirectOffsetOnAxis(self.sidx_u[:, t:t + 1], 0),
                    in_=self.pay_sb[:, 2 * t:2 * t + 2], in_offset=None,
                    bounds_check=c.TOPK - 1, oob_is_err=False), 16)
            g.wait_ge(self.qG.h, self.qG.n)
            self.cC.tag(g.collective_compute("AllReduce", ALU.add,
                                             replica_groups=self.pairs,
                                             ins=[self.selm_in[:, :]],
                                             outs=[self.selm[:, :]]))
            self.CC_SELM = self.cC.n
            g.wait_ge(self.cC.h, self.CC_SELM)
            for i in range(c.OVc):
                self.qG.tag(g.indirect_dma_start(
                    out=self.selmeta_my[:, 2 * i:2 * i + 2], out_offset=None,
                    in_=self.selm[:, :],
                    in_offset=IndirectOffsetOnAxis(self.selidx_u[:, i:i + 1], 0)), 16)
        self.block.gpsimd(g_scatter)
        self.QG_SELM = self.qG.n

        def h_ids(sync):
            sync.wait_ge(self.cC.h, self.CC_SELM)
            for i in range(c.Kc):
                self.qH.tag(sync.dma_start(out=self.ids_all[:, 2 * i:2 * i + 2],
                                           in_=self.selm[128 * i:128 * (i + 1), :]), 16)
        self.block.sync(h_ids)
        self.QH_IDS = self.qH.n

        def v_post(v):
            v.wait_ge(self.qH.h, self.QH_IDS)
            v.wait_ge(self.qG.h, self.QG_SELM)
            for i in range(c.Kc):
                v.tensor_copy(self.allid_u[:, i:i + 1], self.ids_all[:, 2 * i:2 * i + 1])
            for i in range(c.OVc):
                ins = v.tensor_copy(self.myid_u[:, i:i + 1],
                                    self.selmeta_my[:, 2 * i:2 * i + 1])
            self.cV.tag(ins)
            self.CV_GIDS = self.cV.n
        self.block.vector(v_post)
        self.phase_close()

    def _tagX(self, t):
        self.cX.tag(t.matmul(self.ps[5][0:1, 0:1], self.ones_row[0:1, 0:1],
                             self.sc1[0:1, 0:1], start=True, stop=True))

    def _search_T(self, t, k):
        CX0 = 0
        t.wait_ge(self.cY.h, 2 * k + 1)
        t.matmul(self.ps[6][0:64, 0:3], self.ones_row[0:1, 0:64],
                 self.sc1[0:1, 0:3], start=True, stop=True)
        self._tagX(t)
        t.wait_ge(self.cY.h, 2 * k + 2)
        t.matmul(self.ps[7][0:1, 0:1], self.ones_col[0:64, 0:1],
                 self.flag64[0:64, 0:1], start=True, stop=True)
        self._tagX(t)

    def _post_T(self, t):
        c = self.c
        n2 = 2 * c.Tt
        t.wait_ge(self.cY.h, 1)
        t.matmul(self.ps[6][0:32, 0:2], self.ones_row[0:1, 0:32],
                 self.sc1[0:1, 0:2], start=True, stop=True)
        self._tagX(t)
        t.wait_ge(self.cY.h, 2)
        t.matmul(self.ps[6][0:n2, 2:3], self.ltri_sb[0:n2, 0:n2],
                 self.wk_srcs["wk_g"][:, 127:128], start=True, stop=True)
        t.matmul(self.ps[7][0:1, 0:1], self.wk_srcs["gts"][:, 0:1],
                 self.ones_col[0:n2, 0:1], start=True, stop=True)
        t.matmul(self.ps[7][0:1, 1:2], self.wk_srcs["gts"][0:c.Tt, 0:1],
                 self.ones_col[0:c.Tt, 0:1], start=True, stop=True)
        t.matmul(self.ps[7][0:1, 2:3], self.wk_srcs["gts2"][0:c.Tt, 0:1],
                 self.ones_col[0:c.Tt, 0:1], start=True, stop=True)
        self._tagX(t)
        t.wait_ge(self.cY.h, 3)
        t.matmul(self.ps[6][0:32, 3:4], self.ones_row[0:1, 0:32],
                 self.sc1[0:1, 5:6], start=True, stop=True)
        self._tagX(t)
        t.wait_ge(self.cY.h, 4)
        t.matmul(self.ps[7][0:1, 3:4], self.wk_srcs["zs"][:, 0:1],
                 self.ones_col[0:n2, 0:1], start=True, stop=True)
        self._tagX(t)
        t.wait_ge(self.cY.h, 5)
        t.matmul(self.ps[6][0:32, 4:8], self.ones_row[0:1, 0:32],
                 self.sc1[0:1, 11:15], start=True, stop=True)
        self._tagX(t)
        t.wait_ge(self.cY.h, 6)
        t.matmul(self.ps[6][0:c.Tt, 2:3], self.ltri_sb[0:c.Tt, 0:c.Tt],
                 self.wk_srcs["ceq_m"][:, 127:128], start=True, stop=True)
        self._tagX(t)
        t.wait_ge(self.cY.h, 7)
        t.matmul(self.ps[6][0:c.Tt, 2:3], self.ltri_sb[0:c.Tt, 0:c.Tt],
                 self.wk_srcs["m_scan"][:, 127:128], start=True, stop=True)
        self._tagX(t)
        t.wait_ge(self.cY.h, 8)
        t.matmul(self.ps[5][0:1, 0:1], self.ones_row[0:1, 0:1],
                 self.sc1[0:1, 0:1], start=True, stop=True)
        for j, nm in enumerate(["ix_blk", "rw_blk", "m_blk", "rk_scan"]):
            bank = [5, 6, 7, 4][j]
            self.cT.tag(t.transpose(self.ps[bank][:, 0:c.Tt],
                                    self.wk_srcs[nm][0:c.Tt, 0:128],
                                    self.ident_f[0:c.Tt, 0:c.Tt]))

    def bB0(self):
        c = self.c
        self.phase_open()
        g1_rep = self.sb("g1_rep", 128, c.D, F32)
        gat_t = [self.sb(f"gat_t{i}", 128, c.D, F32) for i in range(2)]
        xng_t = [self.sb(f"xng_t{i}", 128, c.D, F32) for i in range(2)]
        hb_t = [self.sb(f"hb_t{i}", 128, c.D, BF16) for i in range(2)]
        xfb_t = [self.sb(f"xfb_t{i}", 128, c.D, BF16) for i in range(2)]
        sqg_t = self.sb("sqg_t", 128, c.D, F32)
        jnk0 = self.sb("jnk0", 128, c.D, F32)
        QH0, QS0, QG0, QGS0 = self.qH.n, self.qS.n, self.qG.n, self.qGs.n
        CV0, CS0 = self.cV.n, self.cS.n
        NG = c.Kc + c.OVc
        qGs = self.qGs

        def b0_sync(sync):
            self.fwait(sync)
            self.qH.tag(sync.dma_start(out=g1_rep[0:1, :], in_=self.g1v[:, :]), 16)
        self.block.sync(b0_sync)
        QH_SMALL = self.qH.n

        def b0_qs(scalar):
            self.fwait(scalar)
            scalar.wait_ge(self.qH.h, QH_SMALL)
            n = 1
            while n < 128:
                k = min(n, 128 - n)
                self.qS.tag(scalar.dma_start(out=g1_rep[n:n + k, :],
                                             in_=g1_rep[0:n, :][0:k, :]), 16)
                scalar.wait_ge(self.qS.h, self.qS.n)
                n += k
        self.block.scalar(b0_qs)
        QS_SMALL = self.qS.n

        def b0_g(g):
            self.fwait(g)
            qGs = self.qGs
            for i in range(NG + 2):
                if i < NG:
                    b = i % 2
                    idxcol = (self.allid_u[:, i:i + 1] if i < c.Kc
                              else self.myid_u[:, i - c.Kc:i - c.Kc + 1])
                    if i >= 2:
                        g.wait_ge(self.cS.h, CS0 + 2 * (i - 1))
                    self.qG.tag(g.indirect_dma_start(
                        out=gat_t[b][:, :], out_offset=None, in_=self.xb[:, :],
                        in_offset=IndirectOffsetOnAxis(idxcol, 0)), 16)
                    g.wait_ge(self.qG.h, self.qG.n)
                if i >= 2:
                    j = i - 2
                    bj = j % 2
                    g.wait_ge(self.cV.h, CV0 + 3 * j + 2)
                    dstd = (self.h_all_scr[128 * j:128 * (j + 1), :] if j < c.Kc
                            else self.h_my_scr[128 * (j - c.Kc):
                                               128 * (j - c.Kc + 1), :])
                    qGs.tag(g.dma_start(out=dstd, in_=hb_t[bj][:, :]), 16)
                    g.wait_ge(self.cV.h, CV0 + 3 * j + 3)
                    if j >= c.Kc:
                        jj = j - c.Kc
                        qGs.tag(g.dma_start(
                            out=self.xf_scr[128 * jj:128 * (jj + 1), :],
                            in_=xfb_t[bj][:, :]), 16)
                    else:
                        qGs.tag(g.dma_start(
                            out=self.xfall_scr[128 * j:128 * (j + 1), :],
                            in_=xfb_t[bj][:, :]), 16)
        self.block.gpsimd(b0_g)

        def b0_scalar(s):
            st = self.st1
            for i in range(NG):
                b = i % 2
                cb, c6 = 4 + 4 * (i % 2), 6 + (i % 2)
                s.wait_ge(self.qG.h, QG0 + 16 * (i + 1))
                if i >= 2:
                    s.wait_ge(self.cV.h, CV0 + 3 * (i - 1))
                s.activation(jnk0[:, :], gat_t[b][:, :], AF.Copy, accum_out=st[:, 0:1])
                s.activation(jnk0[:, :], gat_t[b][:, :], AF.Square, accum_out=st[:, 1:2])
                s.mul(st[:, 0:1], st[:, 0:1], self.inv_d)
                s.mul(st[:, 2:3], st[:, 1:2], self.inv_d)
                s.activation(st[:, 3:4], st[:, 0:1], AF.Square)
                s.mul(st[:, 3:4], st[:, 3:4], -1.0)
                s.add(st[:, 2:3], st[:, 2:3], self.C(1e-5))
                s.add(st[:, 2:3], st[:, 2:3], st[:, 3:4])
                s.mul(st[:, 3:4], st[:, 0:1], -1.0)
                s.activation(st[:, 2:3], st[:, 2:3], AF.Ln)
                s.activation(st[:, cb:cb + 1], st[:, 2:3], AF.Exp, scale=self.C(-0.5))
                self.cS.tag(s.mul(st[:, cb + 1:cb + 2], st[:, cb:cb + 1],
                                  st[:, 3:4]))
                s.wait_ge(self.cV.h, CV0 + 3 * i + 1)
                s.activation(jnk0[:, :], xng_t[b][:, :], AF.Square, accum_out=st[:, 1:2])
                s.mul(st[:, 2:3], st[:, 1:2], self.inv_d)
                s.mul(st[:, 1:2], st[:, 1:2], 1.0)
                s.activation(st[:, 2:3], st[:, 2:3], AF.Ln, bias=self.C(1e-6))
                self.cS.tag(s.activation(st[:, c6:c6 + 1], st[:, 2:3], AF.Exp,
                                         scale=self.C(-0.5)))
        self.block.scalar(b0_scalar)

        def b0_vector(v):
            v.wait_ge(self.qS.h, QS_SMALL)
            st = self.st1
            for i in range(NG):
                b = i % 2
                cb, c6 = 4 + 4 * (i % 2), 6 + (i % 2)
                v.wait_ge(self.cS.h, CS0 + 2 * i + 1)
                if i >= 2:
                    v.wait_ge(qGs.h, QGS0 + 16 * (2 * (i - 1)))
                self.cV.tag(v.tensor_scalar(xng_t[b][:, :], gat_t[b][:, :],
                                            st[:, cb:cb + 1], st[:, cb + 1:cb + 2],
                                            ALU.mult, ALU.add))
                v.wait_ge(self.cS.h, CS0 + 2 * i + 2)
                v.tensor_scalar(sqg_t[:, :], xng_t[b][:, :], st[:, c6:c6 + 1],
                                None, ALU.mult)
                self.cV.tag(v.tensor_tensor(hb_t[b][:, :], sqg_t[:, :], g1_rep[:, :],
                                            ALU.mult), 1)
                self.cV.tag(v.tensor_copy(xfb_t[b][:, :], xng_t[b][:, :]), 1)
        self.block.vector(b0_vector)

        self.phase_close()

    def bB1(self):
        self._bB1grp(0)
        self._bB1grp(1)

    def _bB1grp(self, grp):
        c = self.c
        self.phase_open()
        width = c.TOPK if grp == 0 else c.OV
        src_scr = self.h_all_scr if grp == 0 else self.h_my_scr
        wt = self.wk_t if grp == 0 else self.wq_t
        cos_d = self.cos_all if grp == 0 else self.cos_my
        sin_d = self.sin_all if grp == 0 else self.sin_my
        dst_scr = self.kro_scr if grp == 0 else self.qro_scr
        h_fm = self.sb(f"h_fm{grp}", 128, c.Dc * width, BF16)
        kq = self.sb(f"kq{grp}", 128, c.H * width, BF16)
        kqsw = self.sb(f"kqsw{grp}", 128, c.H * width, BF16)
        kro = self.sb(f"kro1{grp}", 128, c.H * width, BF16)
        cosx = self.sb(f"cosx1{grp}", 128, width, F32)
        sinx = self.sb(f"sinx1{grp}", 128, width, F32)
        rope_t = self.sb(f"rope_t{grp}", 128, width, F32)
        wKQ_b = self.sb(f"wKQ_b{grp}", 128, 2 * c.Dc * 128, BF16)
        KS = c.segs(width)
        NKS = len(KS)
        QH0, QS0, QG0, CT0, CV0, CS0 = (self.qH.n, self.qS.n, self.qG.n,
                                        self.cT.n, self.cV.n, self.cS.n)
        qGs = self.qGs
        QGS0 = qGs.n

        def b1_sync(sync):
            self.fwait(sync)
            self.qH.tag(sync.dma_start(out=cosx[0:64, 0:width], in_=cos_d[:, :]), 16)
            self.qH.tag(sync.dma_start(out=sinx[0:64, 0:width], in_=sin_d[:, :]), 16)
        self.block.sync(b1_sync)
        QH_TABS = self.qH.n

        def b1_qs(scalar):
            self.fwait(scalar)
            for dch in range(c.Dc):
                self.qS.tag(scalar.dma_start(
                    out=h_fm[:, dch * width:(dch + 1) * width],
                    in_=src_scr[0:width, 128 * dch:128 * (dch + 1)],
                    transpose=True), 16)
            scalar.wait_ge(self.qS.h, self.qS.n)
            scalar.wait_ge(self.qH.h, QH_TABS)
            for t in (cosx, sinx):
                self.qS.tag(scalar.dma_start(out=t[64:128, 0:width],
                                             in_=t[0:64, 0:width]), 16)
        self.block.scalar(b1_qs)
        QS_HT = QS0 + 16 * c.Dc

        def b1_g(g):
            self.fwait(g)
            for od in range(c.Dc):
                par = od % 2
                off = par * c.Dc * 128
                if od >= 2:
                    g.wait_ge(self.cT.h, CT0 + NKS * (od - 1))
                kh = max(1, c.Dc // 2)
                for half in range(c.Dc // kh):
                    srcw = wt[half * kh:(half + 1) * kh, od, :, :]
                    dstw = wKQ_b[:, off + half * kh * 128:off + (half + 1) * kh * 128]
                    self.qG.tag(g.dma_start(
                        out=dstw.rearrange("p (k w) -> p k w", k=kh),
                        in_=srcw.rearrange("k p w -> p k w")), 16)
                g.wait_ge(self.qG.h, self.qG.n)
        self.block.gpsimd(b1_g)
        NWD = 2 if c.Dc >= 2 else 1

        def b1_t(t):
            self.fwait(t)
            t.wait_ge(self.qS.h, QS_HT)
            for od in range(c.Dc):
                off = (od % 2) * c.Dc * 128
                t.wait_ge(self.qG.h, QG0 + 16 * NWD * (od + 1))
                for si, (s0, w) in enumerate(KS):
                    bank = (od % 2) * 2 + si % 2
                    if od >= 2:
                        t.wait_ge(self.cS.h, CS0 + NKS * (od - 1) + si + 1)
                    for k in range(c.Dc):
                        ins = t.matmul(
                            self.ps[bank][:, 0:w],
                            wKQ_b[:, off + k * 128:off + (k + 1) * 128],
                            h_fm[:, k * width + s0:k * width + s0 + w],
                            start=(k == 0), stop=(k == c.Dc - 1))
                    self.cT.tag(ins)
        self.block.tensor(b1_t)

        def b1_evac(s):
            for od in range(c.Dc):
                for si, (s0, w) in enumerate(KS):
                    bank = (od % 2) * 2 + si % 2
                    s.wait_ge(self.cT.h, CT0 + NKS * od + si + 1)
                    self.cS.tag(s.activation(kq[:, od * width + s0:
                                                od * width + s0 + w],
                                             self.ps[bank][:, 0:w], AF.Copy))
        self.block.scalar(b1_evac)

        def b1_swaps(scalar):
            scalar.wait_ge(self.cS.h, CS0 + NKS * c.Dc)
            self.qS.tag(scalar.dma_start(out=kqsw[0:64, :], in_=kq[64:128, :]), 16)
            self.qS.tag(scalar.dma_start(out=kqsw[64:128, :], in_=kq[0:64, :]), 16)
        self.block.scalar(b1_swaps)
        QS_SWAP = self.qS.n

        def b1_rope(v):
            v.wait_ge(self.qS.h, QS_SWAP)
            v.tensor_scalar(sinx[0:64, :], sinx[0:64, :], self.Cp(-1.0, 64),
                            None, ALU.mult)
            for h in range(c.H):
                sl = slice(h * width, (h + 1) * width)
                v.tensor_tensor(kro[:, sl], kq[:, sl], cosx[:, :], ALU.mult)
                v.tensor_tensor(rope_t[:, 0:width], kqsw[:, sl], sinx[:, :],
                                ALU.mult)
                self.cV.tag(v.tensor_tensor(kro[:, sl], kro[:, sl],
                                            rope_t[:, 0:width], ALU.add))
        self.block.vector(b1_rope)

        def b1_store(g):
            for h in range(c.H):
                g.wait_ge(self.cV.h, CV0 + h + 1)
                qGs.tag(g.dma_start(out=dst_scr[:, h * width:(h + 1) * width],
                                    in_=kro[:, h * width:(h + 1) * width]), 16)
        self.block.gpsimd(b1_store)
        self.phase_close()

    def bB2(self):
        c = self.c
        self.phase_open()
        hall_fm = self.sb("hall_fm3", 128, c.Dc * c.TOPK, BF16)
        wv_b = self.sb("wv_b", 128, c.Dc * 1024, BF16)
        vstg = [self.sb(f"vstg{i}", 128, 1024, BF16) for i in range(2)]
        QS0, QG0, CT0, CS0 = self.qS.n, self.qG.n, self.cT.n, self.cS.n
        qGs = self.qGs
        QGS0 = qGs.n
        ODH = max(1, c.D // 1024)
        ODW = min(c.D, 1024)
        VS = c.segs(ODW)

        def b2_qs(scalar):
            self.fwait(scalar)
            for dch in range(c.Dc):
                self.qS.tag(scalar.dma_start(
                    out=hall_fm[:, dch * c.TOPK:(dch + 1) * c.TOPK],
                    in_=self.h_all_scr[0:c.TOPK, 128 * dch:128 * (dch + 1)],
                    transpose=True), 16)
        self.block.scalar(b2_qs)
        QS_HT = self.qS.n

        def b2_g(g):
            self.fwait(g)
            for oh in range(ODH):
                for dch in range(c.Dc):
                    if oh > 0 and dch == 0:
                        g.wait_ge(self.cT.h, CT0 + oh * c.Kc)
                    self.qG.tag(g.dma_start(
                        out=wv_b[:, dch * ODW:(dch + 1) * ODW],
                        in_=self.wv_m[128 * dch:128 * (dch + 1),
                                      oh * ODW:(oh + 1) * ODW]), 16)
                g.wait_ge(self.qG.h, self.qG.n)
                for kt in range(c.Kc):
                    par = kt % 2
                    g.wait_ge(self.cS.h, CS0 + oh * c.Kc + kt + 1)
                    qGs.tag(g.dma_start(
                        out=self.v_scr[128 * kt:128 * (kt + 1),
                                       oh * ODW:(oh + 1) * ODW],
                        in_=vstg[par][:, 0:ODW]), 16)
        self.block.gpsimd(b2_g)

        def b2_t(t):
            self.fwait(t)
            t.wait_ge(self.qS.h, QS_HT)
            for oh in range(ODH):
                t.wait_ge(self.qG.h, QG0 + 16 * c.Dc * (oh + 1))
                for kt in range(c.Kc):
                    par = kt % 2
                    if kt >= 2:
                        t.wait_ge(self.cS.h, CS0 + (oh * c.Kc + kt - 1))
                    for dch in range(c.Dc):
                        for si, (s0, w) in enumerate(VS):
                            ins = t.matmul(self.ps[par * 2 + si][:, 0:w],
                                           hall_fm[:, dch * c.TOPK + kt * 128:
                                                   dch * c.TOPK + (kt + 1) * 128],
                                           wv_b[:, dch * ODW + s0:dch * ODW + s0 + w],
                                           start=(dch == 0), stop=(dch == c.Dc - 1))
                    self.cT.tag(ins)
        self.block.tensor(b2_t)

        def b2_evac(s):
            for oh in range(ODH):
                for kt in range(c.Kc):
                    par = kt % 2
                    s.wait_ge(self.cT.h, CT0 + oh * c.Kc + kt + 1)
                    if oh * c.Kc + kt >= 2:
                        s.wait_ge(self.qGs.h, QGS0 + 16 * (oh * c.Kc + kt - 1))
                    for si, (s0, w) in enumerate(VS):
                        if si == len(VS) - 1:
                            self.cS.tag(s.activation(vstg[par][:, s0:s0 + w],
                                                     self.ps[par * 2 + si][:, 0:w],
                                                     AF.Copy))
                        else:
                            s.activation(vstg[par][:, s0:s0 + w],
                                         self.ps[par * 2 + si][:, 0:w], AF.Copy)
        self.block.scalar(b2_evac)

        self.phase_close()

    def bB3(self):
        c = self.c
        self.phase_open()
        kro = self.sb("kro3", 128, c.H * c.TOPK, BF16)
        qro = self.sb("qro3", 128, c.H * c.OV, BF16)
        v_tm = self.sb("v_tm3", 128, c.Kc * c.D, BF16)
        A_sb = self.sb("A_sb", 128, c.Kc * c.OV, BF16)
        o_tm = self.sb("o_tm", 128, c.OVc * c.D, F32)
        o_fm = self.sb("o_fm", 128, c.Dc * c.OV, BF16)
        mask_bf = self.sb("mask_bf", 128, c.OVc * c.TOPK, BF16)
        OS = c.segs(c.OV)
        QH0, QG0, CT0, CV0, CS0 = (self.qH.n, self.qG.n, self.cT.n, self.cV.n,
                                   self.cS.n)
        qGs = self.qGs
        QGS0 = qGs.n
        NSC = len(OS)
        SC = NSC * c.Kc
        HT = SC + 1 + c.OVc
        CVH = c.OVc

        def b3_sync(sync):
            self.fwait(sync)
            self.qH.tag(sync.dma_start(out=kro[:, :], in_=self.kro_scr[:, :]), 16)
            self.qH.tag(sync.dma_start(out=qro[:, :], in_=self.qro_scr[:, :]), 16)
            for kt in range(c.Kc):
                self.qH.tag(sync.dma_start(out=v_tm[:, kt * c.D:(kt + 1) * c.D],
                                           in_=self.v_scr[128 * kt:128 * (kt + 1), :]), 16)
        self.block.sync(b3_sync)
        QH_L = self.qH.n

        def b3_g(g):
            self.fwait(g)
            for i in range(c.OVc):
                self.qG.tag(g.dma_start(out=mask_bf[:, i * c.TOPK:(i + 1) * c.TOPK],
                                        in_=self.maskmy[i * 128:(i + 1) * 128, :]), 16)
        self.block.gpsimd(b3_g)
        QG_M = self.qG.n

        def b3_vpre(v):
            v.wait_ge(self.qG.h, QG_M)
            self.cV.tag(v.tensor_single_scalar(mask_bf[:, :], mask_bf[:, :],
                                               self.C(SQRT128), ALU.mult))
        self.block.vector(b3_vpre)
        CV1 = self.cV.n

        def b3_t(t):
            self.fwait(t)
            t.wait_ge(self.qH.h, QH_L)
            t.wait_ge(self.cV.h, CV1)
            for h in range(c.H):
                for ktc in range(c.Kc):
                    if h + ktc > 0:
                        t.wait_ge(self.cS.h, CS0 + NSC * (h * c.Kc + ktc))
                    for si, (s0, w) in enumerate(OS):
                        ins = t.matmul(self.ps[si][:, 0:w],
                                       kro[:, h * c.TOPK + ktc * 128:
                                           h * c.TOPK + (ktc + 1) * 128],
                                       qro[:, h * c.OV + s0:h * c.OV + s0 + w],
                                       start=True, stop=False)
                        nsub = w // 128
                        for j in range(nsub):
                            i = (s0 // 128) + j
                            last = (j == nsub - 1)
                            ins = t.matmul(
                                self.ps[si][:, 128 * j:128 * (j + 1)],
                                mask_bf[:, i * c.TOPK + ktc * 128:
                                        i * c.TOPK + (ktc + 1) * 128],
                                self.ident_bf[:, :],
                                start=False, stop=last)
                        self.cT.tag(ins)
                t.wait_ge(self.cS.h, CS0 + NSC * c.Kc * (h + 1))
                for ktc in range(c.Kc):
                    for si, (s0, w) in enumerate(OS):
                        ins = t.matmul(self.ps[4 + si][0:c.H, 0:w],
                                       self.zoh[:, h * c.H:(h + 1) * c.H],
                                       A_sb[:, ktc * c.OV + s0:ktc * c.OV + s0 + w],
                                       start=(h == 0 and ktc == 0),
                                       stop=(h == c.H - 1 and ktc == c.Kc - 1))
                self.cT.tag(ins)
                for qt in range(c.OVc):
                    flat = h * c.OVc + qt
                    if flat >= 2:
                        pf = flat - 2
                        t.wait_ge(self.cV.h, CV1 + CVH * (pf // c.OVc)
                                  + (pf % c.OVc) + 1)
                    for ktc in range(c.Kc):
                        ins = t.matmul(self.ps[2 + qt % 2][:, 0:128],
                                       A_sb[:, ktc * c.OV + 128 * qt:
                                           ktc * c.OV + 128 * (qt + 1)],
                                       v_tm[:, ktc * c.D + h * 128:
                                           ktc * c.D + (h + 1) * 128],
                                       start=(ktc == 0), stop=(ktc == c.Kc - 1))
                    self.cT.tag(ins)
            for qt in range(c.OVc):
                t.wait_ge(self.cV.h, CV1 + CVH * c.H + 1)
                self.cT.tag(t.transpose(self.ps[6][:, 0:c.H],
                                        self.zacc[0:c.H, 128 * qt:128 * (qt + 1)],
                                        self.ident_f[0:c.H, 0:c.H]))
            for qt in range(c.OVc):
                for dch in range(c.Dc):
                    flat = qt * c.Dc + dch
                    t.wait_ge(self.cV.h, self.CV_ZDIV0 + c.H * c.OVc)
                    if flat >= 4:
                        t.wait_ge(self.cV.h, self.CV_OT0 + flat - 3)
                    self.cT.tag(t.transpose(
                        self.ps[flat % 4][:, 0:128],
                        o_tm[:, qt * c.D + dch * 128:qt * c.D + (dch + 1) * 128],
                        self.ident_f[:, :]))
        def b3_s(s):
            for h in range(c.H):
                for ktc in range(c.Kc):
                    if h > 0:
                        s.wait_ge(self.cT.h, CT0 + HT * h)
                    s.wait_ge(self.cT.h, CT0 + HT * h + NSC * (ktc + 1))
                    for si, (s0, w) in enumerate(OS):
                        ins = s.activation(A_sb[:, ktc * c.OV + s0:ktc * c.OV + s0 + w],
                                           self.ps[si][:, 0:w], AF.Exp,
                                           scale=self.C(RSQRT128), bias=self.C(CEXP))
                        self.cS.tag(ins)
        self.CV_ZDIV0 = CV1 + CVH * c.H + 2
        self.CV_OT0 = self.CV_ZDIV0 + c.H * c.OVc
        self.block.tensor(b3_t)
        self.block.scalar(b3_s)

        def b3_v(v):
            for h in range(c.H):
                for qt in range(c.OVc):
                    v.wait_ge(self.cT.h, CT0 + HT * h + SC + 1 + qt + 1)
                    self.cV.tag(v.tensor_copy(
                        o_tm[:, qt * c.D + h * 128:qt * c.D + (h + 1) * 128],
                        self.ps[2 + qt % 2][:, 0:128]))
            v.wait_ge(self.cT.h, CT0 + HT * c.H)
            ins = v.tensor_copy(self.zacc[0:c.H, 0:min(512, c.OV)],
                                self.ps[4][0:c.H, 0:min(512, c.OV)])
            if c.OV > 512:
                ins = v.tensor_copy(self.zacc[0:c.H, 512:c.OV],
                                    self.ps[5][0:c.H, 0:c.OV - 512])
            self.cV.tag(ins)
            for qt in range(c.OVc):
                v.wait_ge(self.cT.h, CT0 + HT * c.H + qt + 1)
                v.tensor_copy(self.zrec_sb[:, qt * c.H:(qt + 1) * c.H],
                              self.ps[6][:, 0:c.H])
            v.reciprocal(self.zrec_sb[:, 0:c.OVc * c.H],
                         self.zrec_sb[:, 0:c.OVc * c.H])
            self.cV.tag(v.tensor_copy(self.zrec_sb[:, 0:1], self.zrec_sb[:, 0:1]))
            for qt in range(c.OVc):
                for h in range(c.H):
                    self.cV.tag(v.tensor_scalar(
                        o_tm[:, qt * c.D + h * 128:qt * c.D + (h + 1) * 128],
                        o_tm[:, qt * c.D + h * 128:qt * c.D + (h + 1) * 128],
                        self.zrec_sb[:, qt * c.H + h:qt * c.H + h + 1], None,
                        ALU.mult))
            for qt in range(c.OVc):
                for dch in range(c.Dc):
                    flat = qt * c.Dc + dch
                    v.wait_ge(self.cT.h, CT0 + HT * c.H + c.OVc + flat + 1)
                    self.cV.tag(v.tensor_copy(
                        o_fm[:, dch * c.OV + qt * 128:dch * c.OV + (qt + 1) * 128],
                        self.ps[flat % 4][:, 0:128]))
        self.block.vector(b3_v)

        def b3_store(g):
            g.wait_ge(self.cV.h, self.cV.n)
            qGs.tag(g.dma_start(out=self.o_fm_scr[:, :], in_=o_fm[:, :]), 16)
        self.block.gpsimd(b3_store)
        self.phase_close()

    def bB4(self):
        c = self.c
        self.phase_open()
        o_fm = self.sb("o_fm4", 128, c.Dc * c.OV, BF16)
        att_fm = self.sb("att_fm", 128, c.Dc * c.OV, F32)
        xf_sb = self.sb("xf_sb", 128, c.OVc * c.D, BF16)
        xa_t = self.sb("xa_t", 128, c.D, F32)
        db_t = self.sb("db_t", 128, c.D, BF16)
        wo_b = self.sb("wo_b", 128, 2 * c.Dc * 128, BF16)
        OS = c.segs(c.OV)
        QH0, QG0, CT0, CV0, CS0 = (self.qH.n, self.qG.n, self.cT.n, self.cV.n,
                                   self.cS.n)
        qGs = self.qGs
        QGS0 = qGs.n

        def b4_sync(sync):
            self.fwait(sync)
            self.qH.tag(sync.dma_start(out=o_fm[:, :], in_=self.o_fm_scr[:, :]), 16)
            sync.wait_ge(self.qH.h, self.qH.n)
            for j in range(c.OVc):
                self.qH.tag(sync.dma_start(out=xf_sb[:, j * c.D:(j + 1) * c.D],
                                           in_=self.xf_scr[128 * j:128 * (j + 1), :]), 16)
        self.block.sync(b4_sync)
        QH_L = self.qH.n

        def b4_g(g):
            self.fwait(g)
            for od in range(c.Dc):
                par = od % 2
                off = par * c.Dc * 128
                if od >= 2:
                    g.wait_ge(self.cT.h, CT0 + len(OS) * (od - 1))
                kh = max(1, c.Dc // 2)
                for half in range(c.Dc // kh):
                    srcw = self.wo_t[half * kh:(half + 1) * kh, od, :, :]
                    dstw = wo_b[:, off + half * kh * 128:off + (half + 1) * kh * 128]
                    self.qG.tag(g.dma_start(
                        out=dstw.rearrange("p (k w) -> p k w", k=kh),
                        in_=srcw.rearrange("k p w -> p k w")), 16)
                g.wait_ge(self.qG.h, self.qG.n)
        self.block.gpsimd(b4_g)
        NWD = 2 if c.Dc >= 2 else 1

        def b4_t(t):
            self.fwait(t)
            t.wait_ge(self.qH.h, QH0 + 16)
            for od in range(c.Dc):
                off = (od % 2) * c.Dc * 128
                t.wait_ge(self.qG.h, QG0 + 16 * NWD * (od + 1))
                for si, (s0, w) in enumerate(OS):
                    bank = (od % 2) * 2 + si
                    if od >= 2:
                        t.wait_ge(self.cS.h, CS0 + len(OS) * (od - 1) + si + 1)
                    for k in range(c.Dc):
                        ins = t.matmul(self.ps[bank][:, 0:w],
                                       wo_b[:, off + k * 128:off + (k + 1) * 128],
                                       o_fm[:, k * c.OV + s0:k * c.OV + s0 + w],
                                       start=(k == 0), stop=(k == c.Dc - 1))
                    self.cT.tag(ins)
            t.wait_ge(self.cS.h, CS0 + len(OS) * c.Dc)
            for qt in range(c.OVc):
                for dch in range(c.Dc):
                    flat = qt * c.Dc + dch
                    if flat >= 4:
                        t.wait_ge(self.cV.h, CV0 + 2 * (flat - 3))
                    self.cT.tag(t.transpose(
                        self.ps[4 + flat % 4][:, 0:128],
                        att_fm[:, dch * c.OV + qt * 128:dch * c.OV + (qt + 1) * 128],
                        self.ident_f[:, :]))
        self.block.tensor(b4_t)

        def b4_s(s):
            for od in range(c.Dc):
                for si, (s0, w) in enumerate(OS):
                    bank = (od % 2) * 2 + si
                    s.wait_ge(self.cT.h, CT0 + len(OS) * od + si + 1)
                    self.cS.tag(s.activation(att_fm[:, od * c.OV + s0:
                                                    od * c.OV + s0 + w],
                                             self.ps[bank][:, 0:w], AF.Copy))
        self.block.scalar(b4_s)

        def b4_v(v):
            v.wait_ge(self.qH.h, QH_L)
            NT = len(OS) * c.Dc
            for qt in range(c.OVc):
                for dch in range(c.Dc):
                    flat = qt * c.Dc + dch
                    v.wait_ge(self.cT.h, CT0 + NT + flat + 1)
                    if qt >= 1 and dch == 0:
                        v.wait_ge(qGs.h, QGS0 + 16 * 2 * qt)
                    sl = slice(dch * 128, (dch + 1) * 128)
                    v.tensor_tensor(xa_t[:, sl], self.ps[4 + flat % 4][:, 0:128],
                                    xf_sb[:, qt * c.D + dch * 128:
                                          qt * c.D + (dch + 1) * 128], ALU.add)
                    self.cV.tag(v.tensor_scalar(
                        db_t[:, sl], self.ps[4 + flat % 4][:, 0:128],
                        self.selmeta_my[:, 2 * qt + 1:2 * qt + 2], None, ALU.mult), 2)
        self.block.vector(b4_v)

        def b4_store(g):
            for qt in range(c.OVc):
                g.wait_ge(self.cV.h, CV0 + 2 * c.Dc * (qt + 1))
                qGs.tag(g.dma_start(out=self.xa_scr[128 * qt:128 * (qt + 1), :],
                                    in_=xa_t[:, :]), 16)
                qGs.tag(g.dma_start(out=self.dblk_scr[128 * qt:128 * (qt + 1), :],
                                    in_=db_t[:, :]), 16)
        self.block.gpsimd(b4_store)
        self.phase_close()

    def bB5(self):
        c = self.c
        self.phase_open()
        g2_rep = self.sb("g2_rep", 128, c.D, F32)
        xa_t = [self.sb(f"xa5_{i}", 128, c.D, BF16) for i in range(2)]
        h2b_t = [self.sb(f"h2b5_{i}", 128, c.D, BF16) for i in range(2)]
        sq5 = self.sb("sq5", 128, c.D, F32)
        jnk5 = self.sb("jnk5", 128, c.D, F32)
        QH0, CV0, CS0 = self.qH.n, self.cV.n, self.cS.n
        qGs = self.qGs
        QGS0 = qGs.n

        def b5_sync(sync):
            self.fwait(sync)
            self.qH.tag(sync.dma_start(out=g2_rep[0:1, :], in_=self.g2v[:, :]), 16)
            for qt in range(c.OVc):
                b = qt % 2
                if qt >= 2:
                    sync.wait_ge(self.cV.h, CV0 + 2 * (qt - 1))
                self.qH.tag(sync.dma_start(out=xa_t[b][:, :],
                                           in_=self.xa_scr[128 * qt:128 * (qt + 1), :]), 16)
                sync.wait_ge(self.qH.h, self.qH.n)
        self.block.sync(b5_sync)

        def b5_qs(scalar):
            self.fwait(scalar)
            scalar.wait_ge(self.qH.h, QH0 + 16)
            n = 1
            while n < 128:
                k = min(n, 128 - n)
                self.qS.tag(scalar.dma_start(out=g2_rep[n:n + k, :],
                                             in_=g2_rep[0:n, :][0:k, :]), 16)
                scalar.wait_ge(self.qS.h, self.qS.n)
                n += k
        self.block.scalar(b5_qs)
        QS_G2 = self.qS.n

        def b5_s(s):
            st = self.st1
            for qt in range(c.OVc):
                b = qt % 2
                c6 = 6 + (qt % 2)
                s.wait_ge(self.qH.h, QH0 + 16 * (qt + 2))
                if qt >= 2:
                    s.wait_ge(self.cV.h, CV0 + 2 * (qt - 1))
                s.activation(jnk5[:, :], xa_t[b][:, :], AF.Square, accum_out=st[:, 1:2])
                s.mul(st[:, 2:3], st[:, 1:2], self.inv_d)
                s.mul(st[:, 1:2], st[:, 1:2], 1.0)
                s.activation(st[:, 2:3], st[:, 2:3], AF.Ln, bias=self.C(1e-6))
                self.cS.tag(s.activation(st[:, c6:c6 + 1], st[:, 2:3], AF.Exp,
                                         scale=self.C(-0.5)))
        self.block.scalar(b5_s)

        def b5_v(v):
            v.wait_ge(self.qS.h, QS_G2)
            for qt in range(c.OVc):
                b = qt % 2
                v.wait_ge(self.cS.h, CS0 + qt + 1)
                if qt >= 2:
                    v.wait_ge(qGs.h, QGS0 + 16 * (qt - 1))
                v.tensor_scalar(sq5[:, :], xa_t[b][:, :],
                                self.st1[:, 6 + qt % 2:7 + qt % 2], None, ALU.mult)
                self.cV.tag(v.tensor_tensor(h2b_t[b][:, :], sq5[:, :], g2_rep[:, :],
                                            ALU.mult), 2)
        self.block.vector(b5_v)

        def b5_store(g):
            for qt in range(c.OVc):
                b = qt % 2
                g.wait_ge(self.cV.h, CV0 + 2 * (qt + 1))
                qGs.tag(g.dma_start(out=self.h2b_scr[128 * qt:128 * (qt + 1), :],
                                    in_=h2b_t[b][:, :]), 16)
        self.block.gpsimd(b5_store)
        self.phase_close()

    def bB6(self):
        c = self.c
        self.phase_open()
        OS = c.segs(c.OV)
        NOS = len(OS)
        NG2 = c.ng2_force or (1 if c.Fc * c.OV * 2 <= 49152 else 2)
        assert c.Fc % NG2 == 0
        FG = c.Fc // NG2
        DQ = (c.Dc + 3) // 4
        nch_l = [min(4, c.Dc - 4 * dq) for dq in range(DQ)]
        nch_total = sum(nch_l)
        h2bfm = self.sb("h2bfm", 128, c.Dc * c.OV, BF16)
        prod_f = self.sb("prod_f", 128, FG * c.OV, BF16)
        ffn_fm = self.sb("ffn_fm", 128, c.Dc * c.OV, F32)
        dblk = self.sb("dblk6", 128, c.OVc * c.D, BF16)
        xw_t = self.sb("xw_t", 128, c.D, BF16)
        u_t = self.sb("u_t", 128, c.OV, BF16)
        wF_b = self.sb("wF_b", 128, 4 * c.Dc * 128, BF16)
        w2F_b = self.sb("w2F_b", 128, 2 * 4 * 128, BF16)
        QH0, QS0, QG0, CT0, CV0, CS0 = (self.qH.n, self.qS.n, self.qG.n,
                                        self.cT.n, self.cV.n, self.cS.n)
        qGs = self.qGs
        QGS0 = qGs.n
        NWF = 2 * (2 if c.Dc >= 2 else 1)

        GT = 2 * NOS * FG + DQ * FG
        def ct_ug(grp, fl, sub, si):
            return CT0 + GT * grp + 2 * NOS * fl + sub * NOS + si + 1
        def ct_ug_end(F):
            return CT0 + GT * (F // FG) + 2 * NOS * ((F % FG) + 1)
        def ct_f2(grp, flat2):
            return CT0 + GT * grp + 2 * NOS * FG + flat2 + 1
        CT_TP0 = CT0 + GT * NG2
        def cs_base(grp):
            return CS0 + (2 * NOS * FG) * grp + (nch_total * NOS if grp >= 1 else 0)
        def cs_ug_end(F):
            return cs_base(F // FG) + 2 * NOS * ((F % FG) + 1)
        CS_F2COPY0 = cs_base(0) + 2 * NOS * FG
        CS_END = cs_base(NG2 - 1) + 2 * NOS * FG + (nch_total * NOS if NG2 == 1 else 0)
        if NG2 == 1:
            CS_F2COPY0 = cs_base(0) + 2 * NOS * FG
        CV_COMB0 = CV0 + c.Fc
        CV_XW0 = CV_COMB0 + (nch_total * NOS if NG2 == 2 else 0)
        def qg_wf(grp, fl):
            return QG0 + 16 * ((NWF + DQ) * FG * grp + NWF * (fl + 1))
        def qg_w2(grp, flat2):
            return QG0 + 16 * ((NWF + DQ) * FG * grp + NWF * FG + flat2 + 1)
        self.CV_XW0 = CV_XW0

        def b6_sync(sync):
            self.fwait(sync)
            for j in range(c.OVc):
                self.qH.tag(sync.dma_start(out=dblk[:, j * c.D:(j + 1) * c.D],
                                           in_=self.dblk_scr[128 * j:128 * (j + 1), :]), 16)
        self.block.sync(b6_sync)
        QH_L = self.qH.n

        def b6_qs(scalar):
            self.fwait(scalar)
            for dch in range(c.Dc):
                self.qS.tag(scalar.dma_start(
                    out=h2bfm[:, dch * c.OV:(dch + 1) * c.OV],
                    in_=self.h2b_scr[0:c.OV, 128 * dch:128 * (dch + 1)],
                    transpose=True), 16)
        self.block.scalar(b6_qs)
        QS_HT = self.qS.n

        def b6_g(g):
            self.fwait(g)
            for grp in range(NG2):
                for fl in range(FG):
                    F = grp * FG + fl
                    par = F % 2
                    off = par * 2 * c.Dc * 128
                    if F >= 2:
                        g.wait_ge(self.cT.h, ct_ug_end(F - 2))
                    kh = max(1, c.Dc // 2)
                    for wtt, sub in ((self.fw1_t, 0), (self.fw3_t, 1)):
                        for half in range(c.Dc // kh):
                            srcw = wtt[half * kh:(half + 1) * kh, F, :, :]
                            dstw = wF_b[:, off + sub * c.Dc * 128 + half * kh * 128:
                                        off + sub * c.Dc * 128 + (half + 1) * kh * 128]
                            self.qG.tag(g.dma_start(
                                out=dstw.rearrange("p (k w) -> p k w", k=kh),
                                in_=srcw.rearrange("k p w -> p k w")), 16)
                    g.wait_ge(self.qG.h, self.qG.n)
                for dq in range(DQ):
                    nch = nch_l[dq]
                    for fl in range(FG):
                        F = grp * FG + fl
                        par = F % 2
                        flat2 = dq * FG + fl
                        if flat2 >= 2:
                            g.wait_ge(self.cT.h, ct_f2(grp, flat2 - 1))
                        elif grp >= 1:
                            g.wait_ge(self.cT.h, CT0 + GT)
                        srcw = self.fw2_t[F, 4 * dq:4 * dq + nch, :, :]
                        dstw = w2F_b[:, par * 4 * 128:par * 4 * 128 + nch * 128]
                        self.qG.tag(g.dma_start(
                            out=dstw.rearrange("p (k w) -> p k w", k=nch),
                            in_=srcw.rearrange("k p w -> p k w")), 16)
                        g.wait_ge(self.qG.h, self.qG.n)
        self.block.gpsimd(b6_g)

        def b6_t(t):
            self.fwait(t)
            t.wait_ge(self.qS.h, QS_HT)
            for grp in range(NG2):
                if grp >= 1:
                    t.wait_ge(self.cS.h, CS_F2COPY0 + nch_total * NOS)
                for fl in range(FG):
                    F = grp * FG + fl
                    off = (F % 2) * 2 * c.Dc * 128
                    t.wait_ge(self.qG.h, qg_wf(grp, fl))
                    if F >= 1:
                        t.wait_ge(self.cS.h, cs_ug_end(F - 1))
                    for sub in range(2):
                        for si, (s0, w) in enumerate(OS):
                            bank = sub * 2 + si
                            for k in range(c.Dc):
                                ins = t.matmul(
                                    self.ps[bank][:, 0:w],
                                    wF_b[:, off + sub * c.Dc * 128 + k * 128:
                                         off + sub * c.Dc * 128 + (k + 1) * 128],
                                    h2bfm[:, k * c.OV + s0:k * c.OV + s0 + w],
                                    start=(k == 0), stop=(k == c.Dc - 1))
                            self.cT.tag(ins)
                t.wait_ge(self.cV.h, CV0 + FG * (grp + 1))
                for dq in range(DQ):
                    nch = nch_l[dq]
                    if dq >= 1:
                        if grp == 0 or NG2 == 1:
                            t.wait_ge(self.cS.h, CS_F2COPY0
                                      + sum(nch_l[:dq]) * NOS)
                        else:
                            t.wait_ge(self.cV.h, CV_COMB0
                                      + sum(nch_l[:dq]) * NOS)
                    for fl in range(FG):
                        F = grp * FG + fl
                        par = F % 2
                        flat2 = dq * FG + fl
                        t.wait_ge(self.qG.h, qg_w2(grp, flat2))
                        for j in range(nch):
                            for si, (s0, w) in enumerate(OS):
                                bank = j if si == 0 else 4 + j
                                ins = t.matmul(
                                    self.ps[bank][:, 0:w],
                                    w2F_b[:, par * 4 * 128 + j * 128:
                                          par * 4 * 128 + (j + 1) * 128],
                                    prod_f[:, fl * c.OV + s0:fl * c.OV + s0 + w],
                                    start=(fl == 0), stop=(fl == FG - 1))
                        self.cT.tag(ins)
            for qt in range(c.OVc):
                for dch in range(c.Dc):
                    flat = qt * c.Dc + dch
                    if NG2 == 2:
                        t.wait_ge(self.cV.h, CV_COMB0 + nch_total * NOS)
                    else:
                        t.wait_ge(self.cS.h, CS_F2COPY0 + nch_total * NOS)
                    if flat >= 4:
                        t.wait_ge(self.cV.h, CV_XW0 + flat - 3)
                    self.cT.tag(t.transpose(
                        self.ps[flat % 4][:, 0:128],
                        ffn_fm[:, dch * c.OV + qt * 128:dch * c.OV + (qt + 1) * 128],
                        self.ident_f[:, :]))
        self.block.tensor(b6_t)

        def b6_s(s):
            for grp in range(NG2):
                for fl in range(FG):
                    F = grp * FG + fl
                    if F >= 1:
                        s.wait_ge(self.cV.h, CV0 + F)
                    if grp >= 1 and fl == 0:
                        s.wait_ge(self.cT.h, CT0 + GT)
                    for si, (s0, w) in enumerate(OS):
                        s.wait_ge(self.cT.h, ct_ug(grp, fl, 0, si))
                        self.cS.tag(s.activation(u_t[:, s0:s0 + w],
                                                 self.ps[si][:, 0:w], AF.Silu))
                    for si, (s0, w) in enumerate(OS):
                        s.wait_ge(self.cT.h, ct_ug(grp, fl, 1, si))
                        self.cS.tag(s.activation(
                            prod_f[:, fl * c.OV + s0:fl * c.OV + s0 + w],
                            self.ps[2 + si][:, 0:w], AF.Copy))
                if grp == 0:
                    for dq in range(DQ):
                        nch = nch_l[dq]
                        s.wait_ge(self.cT.h, ct_f2(0, (dq + 1) * FG - 1))
                        for j in range(nch):
                            for si, (s0, w) in enumerate(OS):
                                bank = j if si == 0 else 4 + j
                                self.cS.tag(s.activation(
                                    ffn_fm[:, (4 * dq + j) * c.OV + s0:
                                           (4 * dq + j) * c.OV + s0 + w],
                                    self.ps[bank][:, 0:w], AF.Copy))
        self.block.scalar(b6_s)

        def b6_v(v):
            for grp in range(NG2):
                for fl in range(FG):
                    F = grp * FG + fl
                    v.wait_ge(self.cS.h, cs_ug_end(F))
                    self.cV.tag(v.tensor_tensor(
                        prod_f[:, fl * c.OV:(fl + 1) * c.OV],
                        prod_f[:, fl * c.OV:(fl + 1) * c.OV],
                        u_t[:, 0:c.OV], ALU.mult))
            if NG2 == 2:
                for dq in range(DQ):
                    nch = nch_l[dq]
                    v.wait_ge(self.cT.h, ct_f2(1, (dq + 1) * FG - 1))
                    for j in range(nch):
                        for si, (s0, w) in enumerate(OS):
                            bank = j if si == 0 else 4 + j
                            self.cV.tag(v.tensor_tensor(
                                ffn_fm[:, (4 * dq + j) * c.OV + s0:
                                       (4 * dq + j) * c.OV + s0 + w],
                                ffn_fm[:, (4 * dq + j) * c.OV + s0:
                                       (4 * dq + j) * c.OV + s0 + w],
                                self.ps[bank][:, 0:w], ALU.add))
            v.wait_ge(self.qH.h, QH_L)
            for qt in range(c.OVc):
                for dch in range(c.Dc):
                    flat = qt * c.Dc + dch
                    v.wait_ge(self.cT.h, CT_TP0 + flat + 1)
                    if qt >= 1 and dch == 0:
                        v.wait_ge(qGs.h, QGS0 + 16 * qt)
                    self.cV.tag(v.scalar_tensor_tensor(
                        xw_t[:, dch * 128:(dch + 1) * 128],
                        self.ps[flat % 4][:, 0:128],
                        self.selmeta_my[:, 2 * qt + 1:2 * qt + 2],
                        dblk[:, qt * c.D + dch * 128:qt * c.D + (dch + 1) * 128],
                        ALU.mult, ALU.add))
        self.block.vector(b6_v)

        def b6_store(g):
            for qt in range(c.OVc):
                g.wait_ge(self.cV.h, self.CV_XW0 + c.Dc * (qt + 1))
                qGs.tag(g.dma_start(out=self.xw_scr[128 * qt:128 * (qt + 1), :],
                                    in_=xw_t[:, :]), 16)
        self.block.gpsimd(b6_store)
        self.phase_close()


    def conv(self):
        c = self.c
        self.phase_open()
        h1_sb = self.sb("h1_sb", 128, c.Rc * c.TH, BF16)
        cw_b = [self.sb(f"cw_b{i}", 128, 4 * c.Rc * 128, BF16) for i in range(2)]
        hc_f = [self.sb(f"hc_f{i}", 128, c.Tc, F32) for i in range(2)]
        hc_bf = [self.sb(f"hc_bf{i}", 128, c.Tc, BF16) for i in range(2)]
        TS = c.segs(c.Tc)
        NTS = len(TS)
        assert NTS <= 4
        QH0, QG0, CT0, CV0, CS0 = (self.qH.n, self.qG.n, self.cT.n, self.cV.n,
                                   self.cS.n)
        qGs = self.qGs
        QGS0 = qGs.n
        kh = max(1, c.Rc // 2)
        rsp = c.segs(c.Rc, kh)
        NWC = 4 * len(rsp)

        def cv_sync(sync):
            self.fwait(sync)
            for r in range(c.Rc):
                self.qH.tag(sync.dma_start(out=h1_sb[:, r * c.TH:(r + 1) * c.TH],
                                           in_=self.h1_scr[r, :, :]), 16)
        self.block.sync(cv_sync)
        QH_L = self.qH.n

        def cv_g(g):
            self.fwait(g)
            self.cC.tag(g.collective_compute("AllGather", ALU.bypass,
                                             replica_groups=self.pairs,
                                             ins=[self.xw_scr[:, :]],
                                             outs=[self.xw_pair[:, :]]))
            self.CC_XW = self.cC.n
            for i in range(c.Rc + 2):
                if i < c.Rc:
                    par = i % 2
                    if i >= 2:
                        g.wait_ge(self.cT.h, CT0 + NTS * (i - 1))
                    for k in range(4):
                        for r0, rn in rsp:
                            srcw = self.convT_t[k, r0:r0 + rn, i, :, :]
                            dstw = cw_b[par][:, (k * c.Rc + r0) * 128:
                                             (k * c.Rc + r0 + rn) * 128]
                            self.qG.tag(g.dma_start(
                                out=dstw.rearrange("p (k w) -> p k w", k=rn),
                                in_=srcw.rearrange("k p w -> p k w")), 16)
                    g.wait_ge(self.qG.h, self.qG.n)
                if i >= 2:
                    j = i - 2
                    bj = j % 2
                    g.wait_ge(self.cV.h, CV0 + 2 * j + 2)
                    qGs.tag(g.dma_start(out=self.h2_scr[128 * j:128 * (j + 1), :],
                                        in_=hc_bf[bj][:, :]), 16)
                    qGs.tag(g.dma_start(out=self.h2f_scr[j, :, :],
                                        in_=hc_f[bj][:, :]), 16)
        self.block.gpsimd(cv_g)

        def cv_t(t):
            self.fwait(t)
            t.wait_ge(self.qH.h, QH_L)
            for ro in range(c.Rc):
                par = ro % 2
                t.wait_ge(self.qG.h, QG0 + 16 * NWC * (ro + 1))
                if ro >= 2:
                    t.wait_ge(self.cS.h, CS0 + NTS * (ro - 1))
                nacc = 4 * c.Rc
                idx = 0
                for k in range(4):
                    for ri in range(c.Rc):
                        lhsT = cw_b[par][:, (k * c.Rc + ri) * 128:
                                         (k * c.Rc + ri + 1) * 128]
                        for si, (s0, w) in enumerate(TS):
                            ins = t.matmul(self.ps[par * 4 + si][:, 0:w], lhsT,
                                           h1_sb[:, ri * c.TH + 2 * k + s0:
                                                 ri * c.TH + 2 * k + s0 + w],
                                           start=(idx == 0), stop=(idx == nacc - 1))
                            if idx == nacc - 1:
                                self.cT.tag(ins)
                        idx += 1
        self.block.tensor(cv_t)

        def cv_s(s):
            for ro in range(c.Rc):
                par = ro % 2
                if ro >= 2:
                    s.wait_ge(self.cV.h, CV0 + 2 * (ro - 1))
                    s.wait_ge(qGs.h, QGS0 + 16 * 2 * (ro - 1))
                for si, (s0, w) in enumerate(TS):
                    s.wait_ge(self.cT.h, CT0 + NTS * ro + si + 1)
                    self.cS.tag(s.activation(hc_f[par][:, s0:s0 + w],
                                             self.ps[par * 4 + si][:, 0:w], AF.Copy))
        self.block.scalar(cv_s)

        def cv_v(v):
            for ro in range(c.Rc):
                par = ro % 2
                v.wait_ge(self.cS.h, CS0 + NTS * (ro + 1))
                if ro >= 2:
                    v.wait_ge(qGs.h, QGS0 + 16 * 2 * (ro - 1))
                self.cV.tag(v.tensor_scalar(hc_f[par][:, :], hc_f[par][:, :],
                                            self.convb_sb[:, ro:ro + 1], None,
                                            ALU.add))
                self.cV.tag(v.tensor_copy(hc_bf[par][:, :], hc_f[par][:, :]))
        self.block.vector(cv_v)
        self.phase_close()

    def ri_scan(self):
        c = self.c
        self.phase_open()
        h2_sb = self.sb("h2_sb", 128, c.Rc * c.Tc, BF16)
        wri_b = [self.sb(f"wri_b{i}", 128, 2 * c.Rc * 128, BF16) for i in range(2)]
        h2f_t = [self.sb(f"h2f_t{i}", 128, c.Tc, F32) for i in range(2)]
        r_f = self.sb("r_f", 128, c.Tc, F32)
        i_f = self.sb("i_f", 128, c.Tc, F32)
        a_f = self.sb("a_f", 128, c.Tc, F32)
        t1 = self.sb("t1_ri", 128, c.Tc, F32)
        t2 = r_f
        hl_f = [self.sb(f"hl_f{i}", 128, c.Tc, F32) for i in range(2)]
        Pb = [self.sb(f"Pb{i}", 128, c.Tc, BF16) for i in range(2)]
        onesT = self.sb("onesT", 128, c.Tc, F32)
        TS = c.segs(c.Tc)
        NTS = len(TS)
        assert NTS <= 4
        SCper = 2 * NTS + 3
        CVper = 3
        QH0, QG0, CT0, CV0, CS0 = (self.qH.n, self.qG.n, self.cT.n, self.cV.n,
                                   self.cS.n)
        qGs = self.qGs
        QGS0 = qGs.n
        khr = max(1, c.Rc // 2)
        rsp = c.segs(c.Rc, khr)
        NWR = 2 * len(rsp)

        def ri_sync(sync):
            self.fwait(sync)
            for r in range(c.Rc):
                self.qH.tag(sync.dma_start(
                    out=h2_sb[:, r * c.Tc:(r + 1) * c.Tc],
                    in_=self.h2_scr[128 * r:128 * (r + 1), :]), 16)
            sync.wait_ge(self.qH.h, self.qH.n)
            for ro in range(c.Rc):
                if ro >= 2:
                    sync.wait_ge(self.cV.h, CV0 + CVper * (ro - 1))
                self.qH.tag(sync.dma_start(out=h2f_t[ro % 2][:, :],
                                           in_=self.h2f_scr[ro, :, :]), 16)
                sync.wait_ge(self.qH.h, self.qH.n)
        self.block.sync(ri_sync)

        def ri_g(g):
            self.fwait(g)
            for i in range(c.Rc + 1):
                if i < c.Rc:
                    par = i % 2
                    if i >= 2:
                        g.wait_ge(self.cT.h, CT0 + 2 * NTS * (i - 1))
                    for gate, wt in ((0, self.waT_t), (1, self.wxT_t)):
                        for r0, rn in rsp:
                            srcw = wt[r0:r0 + rn, i, :, :]
                            dstw = wri_b[par][:, (gate * c.Rc + r0) * 128:
                                              (gate * c.Rc + r0 + rn) * 128]
                            self.qG.tag(g.dma_start(
                                out=dstw.rearrange("p (k w) -> p k w", k=rn),
                                in_=srcw.rearrange("k p w -> p k w")), 16)
                    g.wait_ge(self.qG.h, self.qG.n)
                if i >= 1:
                    j = i - 1
                    bj = j % 2
                    g.wait_ge(self.cV.h, CV0 + CVper * j + 3)
                    qGs.tag(g.dma_start(out=self.h3_scr[128 * j:128 * (j + 1), :],
                                        in_=hl_f[bj][:, :]), 16)
                    qGs.tag(g.dma_start(out=self.P_scr[128 * j:128 * (j + 1), :],
                                        in_=Pb[bj][:, :]), 16)
            g.wait_ge(self.cV.h, CV0 + CVper * c.Rc)
            qGs.tag(g.dma_start(out=self.carry_in[:, :],
                                in_=self.carry_sb[:, 0:c.Rc]), 16)
            g.wait_ge(qGs.h, qGs.n)
            self.cC.tag(g.collective_compute("AllGather", ALU.bypass,
                                             replica_groups=self.pairs,
                                             ins=[self.carry_in[:, :]],
                                             outs=[self.carry_pair[:, :]]))
            self.CC_CARRY = self.cC.n
        self.block.gpsimd(ri_g)

        def ri_t(t):
            self.fwait(t)
            t.wait_ge(self.qH.h, QH0 + 16 * c.Rc)
            for ro in range(c.Rc):
                par = ro % 2
                t.wait_ge(self.qG.h, QG0 + 16 * NWR * (ro + 1))
                for gate in range(2):
                    for ri in range(c.Rc):
                        lhsT = wri_b[par][:, (gate * c.Rc + ri) * 128:
                                          (gate * c.Rc + ri + 1) * 128]
                        for si, (s0, w) in enumerate(TS):
                            if ri == 0 and ro >= 1:
                                t.wait_ge(self.cS.h, CS0 + SCper * (ro - 1)
                                          + gate * NTS + si + 1)
                            ins = t.matmul(self.ps[gate * 4 + si][:, 0:w], lhsT,
                                           h2_sb[:, ri * c.Tc + s0:
                                                 ri * c.Tc + s0 + w],
                                           start=(ri == 0), stop=(ri == c.Rc - 1))
                            if ri == c.Rc - 1:
                                self.cT.tag(ins)
        self.block.tensor(ri_t)

        def ri_s(s):
            for ro in range(c.Rc):
                if ro >= 1:
                    s.wait_ge(self.cV.h, CV0 + CVper * ro)
                for gate in range(2):
                    dst = r_f if gate == 0 else i_f
                    bias = self.ba_sb if gate == 0 else self.bx_sb
                    for si, (s0, w) in enumerate(TS):
                        s.wait_ge(self.cT.h, CT0 + 2 * NTS * ro + gate * NTS + si + 1)
                        self.cS.tag(s.activation(dst[:, s0:s0 + w],
                                                 self.ps[gate * 4 + si][:, 0:w],
                                                 AF.Sigmoid, bias=bias[:, ro:ro + 1]))
                s.wait_ge(self.cV.h, CV0 + CVper * ro + 1)
                self.cS.tag(s.activation(a_f[:, :], t1[:, :], AF.Exp))
                self.cS.tag(s.activation(t2[:, :], t1[:, :], AF.Exp,
                                         scale=self.C(2.0)))
                s.wait_ge(self.cV.h, CV0 + CVper * ro + 2)
                self.cS.tag(s.activation(t2[:, :], t1[:, :], AF.Sqrt))
        self.block.scalar(ri_s)

        def ri_v(v):
            self.fwait(v)
            v.memset(onesT[:, :], 1.0)
            for ro in range(c.Rc):
                bj = ro % 2
                v.wait_ge(self.cS.h, CS0 + SCper * ro + NTS)
                self.cV.tag(v.tensor_scalar(t1[:, :], r_f[:, :],
                                            self.msp8_sb[:, ro:ro + 1], None,
                                            ALU.mult))
                v.wait_ge(self.cS.h, CS0 + SCper * ro + 2 * NTS + 2)
                v.tensor_scalar(t1[:, :], t2[:, :], self.C(-1.0), self.C(1.0),
                                ALU.mult, ALU.add)
                self.cV.tag(v.tensor_scalar_max(t1[:, :], t1[:, :], 0.0))
                v.wait_ge(self.cS.h, CS0 + SCper * (ro + 1))
                v.wait_ge(self.qH.h, QH0 + 16 * (c.Rc + ro + 1))
                if ro >= 2:
                    v.wait_ge(qGs.h, QGS0 + 16 * 2 * (ro - 1))
                v.tensor_tensor(t1[:, :], i_f[:, :], h2f_t[bj][:, :], ALU.mult)
                v.tensor_tensor(t1[:, :], t1[:, :], t2[:, :], ALU.mult)
                v.tensor_tensor_scan(hl_f[bj][:, :], a_f[:, :], t1[:, :], 0.0,
                                     ALU.mult, ALU.add)
                v.tensor_copy(self.carry_sb[:, ro:ro + 1],
                              hl_f[bj][:, c.Tc - 1:c.Tc])
                self.cV.tag(v.tensor_tensor_scan(Pb[bj][:, :], a_f[:, :],
                                                 onesT[:, :], 1.0,
                                                 ALU.mult, ALU.mult))
        self.block.vector(ri_v)
        self.phase_close()

    def w2_assembly(self):
        c = self.c
        self.phase_open()
        w2sb = self.sb("w2sb", 128, c.Rc * c.D, BF16)
        hch = [self.sb(f"hch{i}", 128, c.Rc * 128, BF16) for i in range(2)]
        h3t = [self.sb(f"h3t{i}", 128, c.Rc * 128, BF16) for i in range(2)]
        Pt = [self.sb(f"Pt{i}", 128, c.Rc * 128, BF16) for i in range(2)]
        base_t = [self.sb(f"base_a{i}", 128, c.D, F32) for i in range(2)]
        xnb_t = [self.sb(f"xnb_a{i}", 128, c.D, BF16) for i in range(2)]
        ctr_t = [self.sb(f"ctr_a{i}", 128, c.D, BF16) for i in range(2)]
        out_t = [self.sb(f"out_a{i}", 128, c.D, F32) for i in range(2)]
        rcl = self.sb("rcl", 128, c.Rc, F32)
        gei = self.sb("gei", 128, c.Tt, F32)
        big = self.sb("big_a", 128, c.Tt, F32)
        idxp_u = self.sb("idxp_u", 128, c.Tt, U32)
        DS = c.segs(c.D)
        NDS = len(DS)
        assert NDS <= 4
        QH0, QG0, CT0, CV0, CS0 = (self.qH.n, self.qG.n, self.cT.n, self.cV.n,
                                   self.cS.n)
        qGs = self.qGs
        QGS0 = qGs.n
        CC_XW, CC_CARRY = self.CC_XW, self.CC_CARRY
        PERM = 2 + 2 * c.Rc
        CV_PREP = CV0 + 1

        def wa_sync(sync):
            self.fwait(sync)
            sync.wait_ge(self.cC.h, CC_CARRY)
            self.qH.tag(sync.dma_start(out=rcl[:, :],
                                       in_=self.carry_pair[0:128, :]), 16)
            sync.wait_ge(self.qH.h, self.qH.n)
            for m in range(c.Tt):
                par = m % 2
                if m >= 2:
                    sync.wait_ge(self.cV.h, CV_PREP + 2 * (m - 1))
                self.qH.tag(sync.dma_start(
                    out=base_t[par][:, :],
                    in_=self.base_scr[128 * m:128 * (m + 1), :]), 16)
                self.qH.tag(sync.dma_start(
                    out=xnb_t[par][:, :],
                    in_=self.strip[3 + 128 * m:3 + 128 * (m + 1), :]), 16)
                for ri in range(c.Rc):
                    self.qH.tag(sync.dma_start(
                        out=h3t[par][:, 128 * ri:128 * (ri + 1)],
                        in_=self.h3_scr[128 * ri:128 * (ri + 1),
                                        128 * m:128 * (m + 1)]), 16)
                    self.qH.tag(sync.dma_start(
                        out=Pt[par][:, 128 * ri:128 * (ri + 1)],
                        in_=self.P_scr[128 * ri:128 * (ri + 1),
                                       128 * m:128 * (m + 1)]), 16)
                sync.wait_ge(self.qH.h, self.qH.n)
        self.block.sync(wa_sync)

        def wa_g(g):
            self.fwait(g)
            for ri in range(c.Rc):
                self.qG.tag(g.dma_start(out=w2sb[:, ri * c.D:(ri + 1) * c.D],
                                        in_=self.w2T[128 * ri:128 * (ri + 1), :]),
                            16)
            g.wait_ge(self.qG.h, self.qG.n)
            g.wait_ge(self.cC.h, CC_XW)
            g.wait_ge(self.cV.h, CV_PREP)
            for i in range(c.Tt + 2):
                if i < c.Tt:
                    par = i % 2
                    if i >= 2:
                        g.wait_ge(self.cV.h, CV_PREP + 2 * (i - 1))
                    g.memset(ctr_t[par][:, :], 0.0)
                    self.qG.tag(g.indirect_dma_start(
                        out=ctr_t[par][:, :], out_offset=None,
                        in_=self.xw_pair[:, :],
                        in_offset=IndirectOffsetOnAxis(idxp_u[:, i:i + 1], 0),
                        bounds_check=2 * c.OV - 1, oob_is_err=False), 16)
                    g.wait_ge(self.qG.h, self.qG.n)
                if i >= 2:
                    j = i - 2
                    bj = j % 2
                    g.wait_ge(self.cV.h, CV_PREP + 2 * j + 2)
                    qGs.tag(g.dma_start(out=self.out_d[128 * j:128 * (j + 1), :],
                                        in_=out_t[bj][:, :]), 16)
        self.block.gpsimd(wa_g)

        def wa_t(t):
            self.fwait(t)
            t.wait_ge(self.qG.h, QG0 + 16 * c.Rc)
            for m in range(c.Tt):
                par = m % 2
                t.wait_ge(self.cV.h, CV_PREP + 2 * m + 1)
                if m >= 2:
                    t.wait_ge(self.cV.h, CV_PREP + 2 * (m - 1))
                for ri in range(c.Rc):
                    lhsT = hch[par][:, 128 * ri:128 * (ri + 1)]
                    for si, (s0, w) in enumerate(DS):
                        t.matmul(self.ps[par * 4 + si][:, 0:w], lhsT,
                                 w2sb[:, ri * c.D + s0:ri * c.D + s0 + w],
                                 start=(ri == 0), stop=False)
                for si, (s0, w) in enumerate(DS):
                    self.cT.tag(t.matmul(self.ps[par * 4 + si][:, 0:w],
                                         self.ones_r128_bf[0:1, :],
                                         self.b2_bf[0:1, s0:s0 + w],
                                         start=False, stop=True))
        self.block.tensor(wa_t)

        def wa_v(v):
            v.wait_ge(self.qH.h, QH0 + 16)
            v.tensor_scalar(big[:, :], self.m_pc[:, 0:c.Tt], self.C(-1e6),
                            self.C(1e6), ALU.mult, ALU.add)
            v.tensor_scalar(gei[:, :], self.rank_pc[:, 0:c.Tt],
                            self.C(float(c.OV)), None, ALU.is_ge)
            v.tensor_scalar(gei[:, :], gei[:, :],
                            self.C(float(2 * c.OV - c.TOPK)), None, ALU.mult)
            v.tensor_tensor(gei[:, :], gei[:, :], self.rank_pc[:, 0:c.Tt], ALU.add)
            v.tensor_tensor(gei[:, :], gei[:, :], self.m_pc[:, 0:c.Tt], ALU.mult)
            v.tensor_tensor(gei[:, :], gei[:, :], big[:, :], ALU.add)
            v.tensor_copy(idxp_u[:, :], gei[:, :])
            self.cV.tag(v.tensor_scalar(self.rc_sb[:, 0:c.Rc], rcl[:, :],
                                        self.M_ODD, None, ALU.mult))
            for m in range(c.Tt):
                par = m % 2
                v.wait_ge(self.qH.h, QH0 + 16 * (1 + PERM * (m + 1)))
                if m >= 2:
                    v.wait_ge(self.cT.h, CT0 + NDS * (m - 1))
                for ri in range(c.Rc):
                    ins = v.scalar_tensor_tensor(
                        hch[par][:, 128 * ri:128 * (ri + 1)],
                        Pt[par][:, 128 * ri:128 * (ri + 1)],
                        self.rc_sb[:, ri:ri + 1],
                        h3t[par][:, 128 * ri:128 * (ri + 1)], ALU.mult, ALU.add)
                self.cV.tag(ins)
                v.wait_ge(self.cT.h, CT0 + NDS * (m + 1))
                v.wait_ge(self.qG.h, QG0 + 16 * (c.Rc + m + 1))
                if m >= 2:
                    v.wait_ge(qGs.h, QGS0 + 16 * (m - 1))
                for si, (s0, w) in enumerate(DS):
                    v.tensor_tensor(out_t[par][:, s0:s0 + w],
                                    self.ps[par * 4 + si][:, 0:w],
                                    base_t[par][:, s0:s0 + w], ALU.add)
                ins = v.tensor_tensor(out_t[par][:, :], out_t[par][:, :],
                                      ctr_t[par][:, :], ALU.add)
                ins = v.scalar_tensor_tensor(out_t[par][:, :], xnb_t[par][:, :],
                                             self.rw_pc[:, m:m + 1],
                                             out_t[par][:, :], ALU.mult, ALU.add)
                self.cV.tag(ins)
        self.block.vector(wa_v)
        self.phase_close()

    def fin(self):
        def f(g):
            for s in self.sems:
                if s.n > 0:
                    g.wait_ge(s.h, s.n)
            if "p2s" in self.c.dbg_scr:
                self.qG.tag(g.dma_start(out=self.p2s[0:1, 0:16],
                                        in_=self.sc1[:, :]), 16)
                self.qG.tag(g.dma_start(out=self.p2s[0:32, 16:24],
                                        in_=self.scB[:, :]), 16)
                self.qG.tag(g.dma_start(out=self.p2s[1:2, 0:8],
                                        in_=self.drain_sb[:, :]), 16)
                with self.nc.allow_non_contiguous_dma(reason="debug dump"):
                    self.qG.tag(g.dma_start(out=self.p2s[0:64, 24:25],
                                            in_=self.cnt64[:, :]), 16)
                    self.qG.tag(g.dma_start(out=self.p2s[0:64, 25:26],
                                            in_=self.tau64[:, :]), 16)
                    self.qG.tag(g.dma_start(out=self.p2s[0:64, 26:27],
                                            in_=self.flag64[:, :]), 16)
            if "p2f" in self.c.dbg_scr:
                Tt = self.c.Tt
                for i, t in enumerate((self.m_pc, self.rank_pc, self.rw_pc,
                                       self.idxg_pc, self.idxf,
                                       self.pay_sb)):
                    w = 2 * Tt if i == 5 else Tt
                    self.qG.tag(g.dma_start(
                        out=self.p2f[:, i * Tt:i * Tt + w], in_=t[:, 0:w]), 16)
                self.qG.tag(g.dma_start(out=self.p2u[:, 0:Tt],
                                        in_=self.sidx_u[:, :]), 16)
                self.qG.tag(g.dma_start(out=self.p2u[:, Tt:2 * Tt],
                                        in_=self.idxg_u[:, :]), 16)
            for ap, twin in self.dbg_twins:
                self.qG.tag(g.dma_start(out=twin[:, :], in_=ap[:, :]), 16)
            if self.dbg_twins:
                g.wait_ge(self.qG.h, self.qG.n)
        self.block.gpsimd(f)

    def finalize(self):
        self.es.close()
        return self.nc


def build(c: Cfg):
    b = B(c)
    b.p0()
    b.p1()
    if c.phases >= 2:
        b.p2()
    if c.phases >= 4:
        b.bB0()
        b.bB1()
        b.bB2()
        b.bB3()
        b.bB4()
        b.bB5()
        b.bB6()
    if c.phases >= 5:
        b.conv()
        b.ri_scan()
    if c.phases >= 6:
        b.w2_assembly()
    b.fin()
    b.finalize()
    return b


def make_inmaps(c, x, w1, b1, conv_w, conv_b, wa, ba, wx, bx, lam, w2, b2,
                mask, freqs, router, g1, g2, wq, wk, wv, wo, fw1, fw3, fw2):
    f32 = np.float32

    def tile2(m):
        K, M = m.shape
        return np.ascontiguousarray(
            m.reshape(K // 128, 128, M // 128, 128).swapaxes(1, 2)).astype(f32)

    def pc(vec):
        return np.ascontiguousarray(np.asarray(vec, f32).reshape(-1, 128).T)

    ltri = np.triu(np.ones((32, 32), f32), 1)
    shared = dict(
        w1T_t=tile2(np.ascontiguousarray(np.asarray(w1, f32).T)),
        b1_pc=pc(b1),
        convT_t=np.stack([tile2(np.ascontiguousarray(np.asarray(conv_w, f32)[:, :, k].T))
                          for k in range(4)]),
        convb_pc=pc(conv_b),
        waT_t=tile2(np.ascontiguousarray(np.asarray(wa, f32).T)), ba_pc=pc(ba),
        wxT_t=tile2(np.ascontiguousarray(np.asarray(wx, f32).T)), bx_pc=pc(bx),
        lam_pc=pc(lam),
        w2T=np.ascontiguousarray(np.asarray(w2, f32).T),
        b2v=np.asarray(b2, f32).reshape(1, -1),
        cos_all=np.ascontiguousarray(np.asarray(freqs, f32)[:, :, 0].T),
        sin_all=np.ascontiguousarray(np.asarray(freqs, f32)[:, :, 1].T),
        router_v=np.asarray(router, f32).reshape(1, -1),
        g1v=np.asarray(g1, f32).reshape(1, -1),
        g2v=np.asarray(g2, f32).reshape(1, -1),
        wq_t=tile2(np.asarray(wq, f32)), wk_t=tile2(np.asarray(wk, f32)),
        wv_m=np.asarray(wv, f32), wo_t=tile2(np.asarray(wo, f32)),
        fw1_t=tile2(np.asarray(fw1, f32)), fw3_t=tile2(np.asarray(fw3, f32)),
        fw2_t=tile2(np.asarray(fw2, f32)),
        ident=np.eye(128, dtype=f32),
        iota0=np.arange(128, dtype=f32).reshape(128, 1),
        ltri=ltri,
    )
    x = np.asarray(x, f32)
    mask = np.asarray(mask, f32)
    freqs = np.asarray(freqs, f32)
    in_maps = []
    for core in range(c.ncores):
        p, h = core // 2, core % 2
        qb = (c.TOPK - c.OV) * h
        tb = c.Tc * h
        xp = x[p]
        halo = np.zeros((c.TH, c.D), f32)
        lo_t = tb - 3
        s_lo, s_hi = max(lo_t, 0), min(tb + c.Tc + 3, c.S)
        halo[s_lo - lo_t: s_hi - lo_t] = xp[s_lo:s_hi]
        meta = np.zeros((128, 8), f32)
        meta[:, 0] = float(h)
        meta[:, 1] = float(qb)
        meta[:, 2] = float(tb)
        meta[:, 3] = float(1 - h)
        m = dict(shared)
        m.update(xb=xp, xhalo=halo, meta=meta,
                 maskmy=np.ascontiguousarray(mask[qb:qb + c.OV, :]),
                 cos_my=np.ascontiguousarray(freqs[qb:qb + c.OV, :, 0].T),
                 sin_my=np.ascontiguousarray(freqs[qb:qb + c.OV, :, 1].T))
        in_maps.append(m)
    return in_maps

def kernel(**inputs):
    c = Cfg(8, 4096, 2048, 2560, 8192)
    b = build(c)
    g = lambda n: np.asarray(inputs[n])
    in_maps = make_inmaps(
        c, g("L0_attn_x"), g("L0_attn_w1"), g("L0_attn_b1"), g("L0_attn_conv_w"),
        g("L0_attn_conv_b"), g("L0_attn_wa"), g("L0_attn_ba"), g("L0_attn_wx"),
        g("L0_attn_bx"), g("L0_attn_lam"), g("L0_attn_w2"), g("L0_attn_b2"),
        g("L0_ffn_mask"), g("L0_ffn_freqs_cis"), g("L0_ffn_w_router"),
        g("L0_ffn_g1"), g("L0_ffn_g2"), g("L0_ffn_wq"), g("L0_ffn_wk"),
        g("L0_ffn_wv"), g("L0_ffn_wo"), g("L0_ffn_w1"), g("L0_ffn_w3"),
        g("L0_ffn_w2"))
    res = run_bass_kernel_spmd(b.nc, in_maps, core_ids=list(range(c.ncores)))
    out = np.empty((c.B, c.S, c.D), np.float32)
    for core in range(c.ncores):
        p, h = core // 2, core % 2
        out[p, c.Tc * h:c.Tc * (h + 1)] = res.results[core]["out"]
    return out
